# version 16
# baseline (speedup 1.0000x reference)
import contextlib
import numpy as np
import ml_dtypes
import concourse.bass as bass
import concourse.mybir as mybir
from concourse.bass_utils import run_bass_kernel_spmd

F32 = mybir.dt.float32
BF16 = mybir.dt.bfloat16
FP16 = mybir.dt.float16
U8 = mybir.dt.uint8
AF = mybir.ActivationFunctionType
OP = mybir.AluOpType
AX = mybir.AxisListType.X
BF = ml_dtypes.bfloat16

S, D, H, L, HD = 8192, 2048, 16, 512, 128
NJ = 4
OWN = 2048
NT = S // 128
EPS = 1e-5
ISQ = float(1.0 / np.sqrt(np.float64(L)))
LN_ISQ = float(np.log(1.0 / np.sqrt(np.float64(L))))

ENGS = ("pe", "act", "dve", "sync", "gp")


class Plan:

    def __init__(self):
        self.ev = []
        self.last_w = {}
        self.last_r = {}

    def op(self, eng, fn, reads=(), writes=(), dma=False):
        i = len(self.ev)
        deps = set()
        for k in reads:
            if k in self.last_w:
                deps.add(self.last_w[k])
        for k in writes:
            if k in self.last_w:
                deps.add(self.last_w[k])
            for j in self.last_r.get(k, {}).values():
                deps.add(j)
        self.ev.append(dict(i=i, eng=eng, fn=fn, deps=deps, dma=dma))
        for k in reads:
            self.last_r.setdefault(k, {})[eng] = i
        for k in writes:
            self.last_w[k] = i
            self.last_r[k] = {}
        return i

    def barrier(self, nops):
        bid = len(self.ev)
        pending = list(self.last_w.keys())
        for e in ENGS:
            self.op(e, nops[e], reads=pending, writes=[("bar", bid, e)])
        for e in ENGS:
            self.op(e, None, reads=[("bar", bid, e2) for e2 in ENGS if e2 != e])

    def finalize(self):
        need = [False] * len(self.ev)
        for e in self.ev:
            byeng = {}
            for d in sorted(e["deps"]):
                p = self.ev[d]
                if p["eng"] == e["eng"] == "pe":
                    continue
                if byeng.get(p["eng"], -1) < d:
                    byeng[p["eng"]] = d
            e["waits"] = byeng
            for d in byeng.values():
                need[d] = True
        cnt = {g: 0 for g in ENGS}
        for e in self.ev:
            if e["dma"]:
                need[e["i"]] = True
            if need[e["i"]]:
                e["inc"] = 16 if e["dma"] else 1
                cnt[e["eng"]] += e["inc"]
            else:
                e["inc"] = 0
            e["cum"] = cnt[e["eng"]]
        water = {}
        for e in self.ev:
            wl = []
            for peng in sorted(e["waits"]):
                val = self.ev[e["waits"][peng]]["cum"]
                assert self.ev[e["waits"][peng]]["inc"] > 0
                key = (e["eng"], peng)
                if water.get(key, -1) >= val:
                    continue
                water[key] = val
                wl.append((peng, val))
            e["wl"] = wl
        self.totals = cnt


def build_nc():
    nc = bass.Bass("TRN2", target_bir_lowering=False, debug=False)

    def din(name, shape, dt=F32):
        return nc.dram_tensor(name, shape, dt, kind="ExternalInput").ap()

    x_full = din("x_full", [S, D])
    x_own = din("x_own", [OWN, D])
    Wq = din("Wq", [H, HD, HD])
    Wdkv = din("Wdkv", [H, L, HD])
    Wuk = din("Wuk", [H, L, HD])
    Wuv = din("Wuv", [H, L, HD])
    Wo_h = din("Wo_h", [H, HD, HD])
    Wo = din("Wo", [D, D])
    bo = din("bo", [D])
    gate_w = din("gate_w", [D, 8])
    gate_b = din("gate_b", [8])
    expert_w = din("expert_w", [8, D, D])
    expert_b = din("expert_b", [8, D])
    lngcol = din("lngcol", [128, 64])
    lnbcol = din("lnbcol", [128, 64])
    iota_q = din("iota_q", [128, 512], FP16)
    smask = din("smask", [128, 64], FP16)
    ones_r = din("ones_r", [1, 128], BF16)
    consts = din("consts", [128, 8])

    out_own = nc.dram_tensor("out_own", [OWN, D], F32, kind="ExternalOutput").ap()

    def dint(name, shape, dt=BF16):
        return nc.dram_tensor(name, shape, dt, kind="Internal").ap()

    xn_hbm = dint("xn_hbm", [S, D])
    xnq_hbm = dint("xnq_hbm", [OWN, D])
    oh2_hbm = dint("oh2_hbm", [D, OWN])
    mask_hbm = dint("mask_hbm", [64, 128, 512])
    wq_bf = dint("wq_bf", [H, HD, HD])
    wdkv_bf = dint("wdkv_bf", [H, L, HD])
    wukp_bf = dint("wukp_bf", [H, L, HD])
    wuv_bf = dint("wuv_bf", [H, L, HD])
    woh_bf = dint("woh_bf", [H, HD, HD])
    wo_bf = dint("wo_bf", [D, D])

    ARENA_BYTES = 210752
    backing = nc.alloc_sbuf_tensor("arena", [128, ARENA_BYTES], U8)
    base = nc.lookup_mloc(backing).addr
    cur = [0]

    def place(name, shape, dt, off=None):
        nbytes = int(np.prod(shape[1:])) * mybir.dt.size(dt)
        if off is None:
            off = cur[0]
            cur[0] = (off + nbytes + 31) & ~31
        else:
            assert off + nbytes <= ARENA_BYTES, name
        assert off + nbytes <= ARENA_BYTES, (name, off, nbytes)
        h = nc.virtual_sbuf_tensor(name, list(shape), dt)
        nc._tensor(h.name, h.shape, h.dtype, type="SB")
        mloc = nc.lookup_mloc(h)
        mloc.allocated = True
        mloc.addr = base + off
        return h.ap()

    A1, A2, A3, A4, A5, A6 = 0, 65536, 98304, 114688, 131072, 147456
    ST = 163840
    cur[0] = ST
    x_t = [place(f"x_t{n}", [128, D], F32, A1 + 8192 * n) for n in range(4)]
    xnout = [place(f"xnout{n}", [128, D], BF16, A1 + 32768 + 4096 * n)
             for n in range(4)]
    cT = place("cT", [128, 4, S], BF16, A1)
    xnT = [place("xnT0", [128, S], BF16, A2), place("xnT1", [128, S], BF16, A2 + 16384)]
    vt = place("vt", [128, NT, 128], BF16, A3)
    wdkvc = place("wdkvc", [128, H, L], BF16, A4)
    absorbed = place("absorbed", [128, H, L], BF16, A5)
    wuv2g = place("wuv2g", [128, H, L], BF16, A6)
    o = A3
    wukstg = [place("wukstg0", [128, 4, 128], F32, o),
              place("wukstg1", [128, 4, 128], F32, o + 2048)]
    wukpstg = [place("wukpstg0", [128, 4, 128], BF16, o + 4096),
               place("wukpstg1", [128, 4, 128], BF16, o + 5120)]
    wqT = place("wqT", [128, 128], BF16, o + 6144)
    wukpT = place("wukpT", [128, 512], BF16, o + 6400)
    wdkvT = place("wdkvT", [128, 512], BF16, o + 7424)
    wuvT = place("wuvT", [128, 512], BF16, o + 8448)
    wohT = place("wohT", [128, 128], BF16, o + 9472)
    wuv2raw = place("wuv2raw", [128, 512], BF16, o + 9728)
    msum = place("msum", [128, 1], F32, o + 10752)
    mcol = place("mcol", [128, 1], F32, o + 10784)
    xnqT = place("xnqT", [128, 16, D], BF16, A1)
    acc = place("acc", [128, 16, 512], F32, A2)
    oh2sb = place("oh2sb", [128, 16, 512], BF16, A3)
    wot = place("wot", [128, 16, 512], BF16, A4)
    exsl = [place("ex0", [128, 16, 512], BF16, A5), place("ex1", [128, 16, 512], BF16, A6)]

    xqT = [place("xqT0", [128, OWN], BF16), place("xqT1", [128, OWN], BF16)]
    ones_sb = place("ones_sb", [1, 128], BF16)
    cc = place("cc_sb", [128, 8], F32)
    lng_sb = place("lng_sb", [128, 64], F32)
    lnb_sb = place("lnb_sb", [128, 64], F32)
    lnbb_sb = place("lnbb_sb", [128, 64], BF16)
    vbeta = place("vbeta", [128, H], F32)
    iota_sb = place("iota_sb", [128, 512], FP16)
    smask_sb = place("smask_sb", [128, 64], FP16)
    smaskf = place("smaskf", [128, 64], F32)
    ph2s = cur[0]
    w2 = [place(f"w2_{n}", [128, 512], BF16) for n in range(5)]
    w2m = [place(f"w2m{n}", [128, 512], BF16) for n in range(4)]
    qlat = place("qlat", [128, 4, 512], BF16)
    ssq = place("ssq", [128, NT], F32)
    lnv = place("lnv", [128, NT], F32)
    As_ = place("As_", [128, NT], F32)
    lnA = place("lnA", [128, NT], F32)
    invA = place("invA", [128, NT], F32)
    sqdump = place("sqdump", [128, 512], BF16)
    bn6 = place("bn6", [128, 6], F32)
    bn2 = place("bn2", [128, 2], F32)
    rs_row = place("rs_row", [1, 512], F32)
    rowb = place("rowb", [1, 512], BF16)
    rb_sb = place("rb_sb", [128, 512], F32)
    racc = [place("racc0", [128, 512], F32), place("racc1", [128, 512], F32)]
    mring = [place(f"mring{n}", [128, 512], BF16) for n in range(6)]
    o_tmp = place("o_tmp", [128, 512], F32)
    oh2stg = [place("oh2stg0", [128, 512], BF16), place("oh2stg1", [128, 512], BF16)]
    st24 = [place(f"st24_{n}", [128, 24], F32) for n in range(4)]
    st2 = [place(f"st2_{n}", [128, 2], F32) for n in range(4)]
    nm1 = [place(f"nm1_{n}", [128, 1], F32) for n in range(4)]
    lnv1 = [place(f"lnv1_{n}", [128, 1], F32) for n in range(4)]
    rs1 = [place(f"rs1_{n}", [128, 1], F32) for n in range(4)]
    nmrs = [place(f"nmrs_{n}", [128, 1], F32) for n in range(4)]
    ph2e = cur[0]
    assert ph2e <= ARENA_BYTES, ph2e
    cur[0] = ph2s
    g8 = place("g8", [128, 8], F32)
    mx8 = place("mx8", [128, 8], F32)
    mask_sb = place("mask_sb", [128, 16, 8], F32)
    gwb = place("gwb", [128, 16, 8], BF16)
    gbrow = place("gbrow", [1, 8], BF16)
    borow = place("borow", [1, 512], BF16)
    berow = [place("berow0", [1, 512], BF16), place("berow1", [1, 512], BF16)]
    xres = [place("xres0", [128, 512], F32), place("xres1", [128, 512], F32)]
    outstg = [place("outstg0", [128, 512], F32), place("outstg1", [128, 512], F32)]
    relutmp = [place("rlt0", [128, 512], F32), place("rlt1", [128, 512], F32)]
    assert cur[0] <= ph2e, (cur[0], ph2e)

    pb = [nc.alloc_psum_tensor(f"pb{k}", [128, 512], F32).ap() for k in range(8)]

    es = contextlib.ExitStack()
    sem = {g: es.enter_context(nc.semaphore(f"sem_{g}")) for g in ENGS}
    block = es.enter_context(nc.Block())

    P = Plan()
    KW = dict()

    for nm, src, dst in (("wq", Wq, wq_bf), ("wdkv", Wdkv, wdkv_bf),
                         ("wuv", Wuv, wuv_bf), ("woh", Wo_h, woh_bf),
                         ("wo", Wo, wo_bf)):
        P.op("gp", lambda g, s=src, d=dst: g.dma_start(out=d, in_=s),
             writes=[("bf", nm)], dma=True)
    P.op("sync", lambda s: s.dma_start(out=iota_sb, in_=iota_q), writes=["iota"], dma=True)
    P.op("sync", lambda s: s.dma_start(out=smask_sb, in_=smask), writes=["smask"], dma=True)
    P.op("sync", lambda s: s.dma_start(out=lng_sb, in_=lngcol), writes=["lng"], dma=True)
    P.op("sync", lambda s: s.dma_start(out=lnb_sb, in_=lnbcol), writes=["lnb"], dma=True)
    P.op("sync", lambda s: s.dma_start(out=ones_sb, in_=ones_r), writes=["ones"], dma=True)
    P.op("sync", lambda s: s.dma_start(out=cc, in_=consts), writes=["cc"], dma=True)
    P.op("dve", lambda v: v.tensor_copy(out=lnbb_sb, in_=lnb_sb),
         reads=["lnb"], writes=["lnbb"])
    def prep_wdkvc():
        for h in range(H):
            P.op("sync", lambda s, h=h: s.dma_start_transpose(wdkvT, wdkv_bf[h]),
                 reads=[("bf", "wdkv")], writes=["wdkvT"], dma=True)
            P.op("dve", lambda v: v.reduce_sum(out=msum, in_=wdkvT, axis=AX),
                 reads=["wdkvT"], writes=["msum"])
            P.op("dve", lambda v: v.tensor_scalar(out=mcol, in0=msum, scalar1=-1.0 / L,
                                                  scalar2=None, op0=OP.mult),
                 reads=["msum"], writes=["mcol"])
            P.op("dve", lambda v, h=h: v.tensor_scalar(
                out=wdkvc[:, h, :], in0=wdkvT, scalar1=mcol, scalar2=None, op0=OP.add),
                reads=["wdkvT", "mcol"], writes=[("wdkvc", h)])

    def prep_weights():
      for h in range(H):
        i = h % 2
        P.op("gp", lambda g, h=h, i=i: g.dma_start(
            out=wukstg[i], in_=Wuk[h].rearrange("(lc p) j -> p lc j", p=128)),
            writes=[("wukstg", i)], dma=True)
        for lc in range(4):
            P.op("dve", lambda v, h=h, lc=lc, i=i: v.tensor_scalar(
                out=wukpstg[i][:, lc, :], in0=wukstg[i][:, lc, :],
                scalar1=lng_sb[:, h * 4 + lc:h * 4 + lc + 1],
                scalar2=None, op0=OP.mult),
                reads=[("wukstg", i), "lng"], writes=[("wukpstg", i)])
        P.op("gp", lambda g, h=h, i=i: g.dma_start(
            out=wukp_bf[h].rearrange("(lc p) j -> p lc j", p=128), in_=wukpstg[i]),
            reads=[("wukpstg", i)], writes=[("wukp", h)], dma=True)
      for h in range(H):
        P.op("sync", lambda s, h=h: s.dma_start_transpose(wqT, wq_bf[h]),
             reads=[("bf", "wq")], writes=["wqT"], dma=True)
        P.op("sync", lambda s, h=h: s.dma_start_transpose(wukpT, wukp_bf[h]),
             reads=[("wukp", h)], writes=["wukpT"], dma=True)
        P.op("pe", lambda t: t.matmul(pb[4][:], wqT, wukpT, start=True, stop=True),
             reads=["wqT", "wukpT"], writes=["pb4"])
        P.op("dve", lambda v, h=h: v.tensor_copy(out=absorbed[:, h, :], in_=pb[4][:]),
             reads=["pb4"], writes=[("abs", h)])
        P.op("sync", lambda s, h=h: s.dma_start_transpose(wuvT, wuv_bf[h]),
             reads=[("bf", "wuv")], writes=["wuvT"], dma=True)
        P.op("sync", lambda s, h=h: s.dma_start_transpose(wohT, woh_bf[h]),
             reads=[("bf", "woh")], writes=["wohT"], dma=True)
        for lc in range(4):
            P.op("pe", lambda t, lc=lc: t.matmul(
                pb[5][:, 128 * lc:128 * (lc + 1)], wuvT[:, 128 * lc:128 * (lc + 1)],
                wohT, start=True, stop=True),
                reads=["wuvT", "wohT"], writes=["pb5"])
        P.op("dve", lambda v: v.tensor_copy(out=wuv2raw, in_=pb[5][:]),
             reads=["pb5"], writes=["wuv2raw"])
        for lc in range(4):
            P.op("dve", lambda v, h=h, lc=lc: v.tensor_scalar(
                out=wuv2g[:, h, 128 * lc:128 * (lc + 1)],
                in0=wuv2raw[:, 128 * lc:128 * (lc + 1)],
                scalar1=lng_sb[:, h * 4 + lc:h * 4 + lc + 1], scalar2=None, op0=OP.mult),
                reads=["wuv2raw", "lng"], writes=[("wuv2g", h)])
        for lc in range(4):
            P.op("pe", lambda t, h=h, lc=lc: t.matmul(
                pb[4][:, 0:1], wuv2raw[:, 128 * lc:128 * (lc + 1)],
                lnbb_sb[:, h * 4 + lc:h * 4 + lc + 1],
                start=(lc == 0), stop=(lc == 3)),
                reads=["wuv2raw", "lnbb"], writes=["pb4"])
        P.op("dve", lambda v, h=h: v.tensor_copy(out=vbeta[:, h:h + 1], in_=pb[4][:, 0:1]),
             reads=["pb4"], writes=[("vbeta", h)])

    prep_wdkvc()

    def ln_block(blk, src_rows, dst, key):
        i = blk % 4
        P.op("sync", lambda s, r=src_rows, i=i: s.dma_start(out=x_t[i], in_=r),
             writes=[("x", i)], dma=True)
        for q in range(4):
            P.op("dve", lambda v, i=i, q=q: v.bn_stats(
                out=st24[i][:, 6 * q:6 * (q + 1)], in_=x_t[i][:, 512 * q:512 * (q + 1)]),
                reads=[("x", i)], writes=[("st24", i)])
        P.op("dve", lambda v, i=i: v.bn_aggr(out=st2[i], in_=st24[i]),
             reads=[("st24", i)], writes=[("st2", i)])
        P.op("dve", lambda v, i=i: v.tensor_scalar(out=nm1[i], in0=st2[i][:, 0:1],
                                                   scalar1=-1.0, scalar2=None, op0=OP.mult),
             reads=[("st2", i)], writes=[("nm1", i)])
        P.op("act", lambda a, i=i: a.activation(out=lnv1[i], in_=st2[i][:, 1:2], func=AF.Ln,
                                                bias=cc[:, 0:1]),
             reads=[("st2", i), "cc"], writes=[("lnv1", i)])
        P.op("act", lambda a, i=i: a.activation(out=rs1[i], in_=lnv1[i], func=AF.Exp,
                                                scale=cc[:, 1:2]),
             reads=[("lnv1", i), "cc"], writes=[("rs1", i)])
        P.op("act", lambda a, i=i: a.mul(out=nmrs[i], in_=rs1[i], mul=nm1[i]),
             reads=[("rs1", i), ("nm1", i)], writes=[("nmrs", i)])
        P.op("act", lambda a, i=i: a.activation(out=xnout[i], in_=x_t[i], func=AF.Identity,
                                                scale=rs1[i], bias=nmrs[i]),
             reads=[("x", i), ("rs1", i), ("nmrs", i)], writes=[("xnout", i)])
        P.op("gp", lambda g, d=dst, i=i: g.dma_start(out=d, in_=xnout[i]),
             reads=[("xnout", i)], writes=[key], dma=True)

    all_xnq = [("xnq", b) for b in range(16)]

    def load_xnT(h):
        hb = h % 2
        for tg in range(16):
            P.op("sync", lambda s, h=h, hb=hb, tg=tg: s.dma_start_transpose(
                xnT[hb][:, 512 * tg:512 * (tg + 1)],
                xn_hbm[512 * tg:512 * (tg + 1), 128 * h:128 * (h + 1)]),
                reads=[("xn", 4 * tg + n) for n in range(4)],
                writes=[("xnT", hb, tg)], dma=True)

    def load_xqT(h):
        P.op("sync", lambda s, h=h: s.dma_start_transpose(
            xqT[h % 2], xnq_hbm[:, 128 * h:128 * (h + 1)]),
            reads=all_xnq, writes=[("xqT", h % 2)], dma=True)

    for blk in range(NT):
        ln_block(blk, x_full[128 * blk:128 * (blk + 1), :],
                 xn_hbm[128 * blk:128 * (blk + 1), :], ("xn", blk))
    load_xnT(0)
    for blk in range(16):
        ln_block(NT + blk, x_own[128 * blk:128 * (blk + 1), :],
                 xnq_hbm[128 * blk:128 * (blk + 1), :], ("xnq", blk))
    load_xqT(0)
    P.op("dve", lambda v: v.tensor_copy(out=smaskf, in_=smask_sb),
         reads=["smask"], writes=["smaskf"])
    for w in range(64):
        P.op("dve", lambda v, w=w: v.tensor_scalar(
            out=w2[0], in0=iota_sb, scalar1=smaskf[:, w:w + 1], scalar2=None,
            op0=OP.is_ge),
            reads=["iota", "smaskf"], writes=["mstg"])
        P.op("gp", lambda g, w=w: g.dma_start(out=mask_hbm[w], in_=w2[0]),
             reads=["mstg"], writes=[("mhbm", w)], dma=True)
    prep_weights()

    nops = dict(pe=lambda t: t.nop(), act=lambda a: a.nop(), dve=lambda v: v.nop(),
                sync=lambda s: s.nop(), gp=lambda g: g.nop())
    P.barrier(nops)

    deferred = [None]
    for h in range(H):
        hb = h % 2
        xn_h = xnT[hb]

        def stats_mm(tc, h=h, hb=hb, xn_h=xn_h):
            k = 6 + (tc % 2)
            P.op("pe", lambda t, tc=tc, k=k, h=h: t.matmul(
                pb[k][:], xn_h[:, 128 * tc:128 * (tc + 1)], wdkvc[:, h, :],
                start=True, stop=True),
                reads=[("xnT", hb, tc // 4), ("wdkvc", h)], writes=[f"pb{k}"])
            if tc % 8 < 6:
                P.op("act", lambda a, tc=tc, k=k: a.activation(
                    out=sqdump, in_=pb[k][:], func=AF.Square, scale=cc[:, 7:8],
                    accum_out=ssq[:, tc:tc + 1]),
                    reads=[f"pb{k}", "cc"], writes=["ssq", "sqdump"])
            else:
                P.op("dve", lambda v, k=k: v.bn_stats(out=bn6, in_=pb[k][:]),
                     reads=[f"pb{k}"], writes=["bn6"])
                P.op("dve", lambda v: v.bn_aggr(out=bn2, in_=bn6),
                     reads=["bn6"], writes=["bn2"])
                P.op("dve", lambda v, tc=tc: v.tensor_copy(
                    out=ssq[:, tc:tc + 1], in_=bn2[:, 1:2]),
                    reads=["bn2"], writes=["ssq"])

        def ct_mm(tg, h=h, hb=hb, xn_h=xn_h):
            for lc in range(4):
                k = lc % 2
                P.op("pe", lambda t, lc=lc, tg=tg, k=k, h=h: t.matmul(
                    pb[k][:], wdkvc[:, h, 128 * lc:128 * (lc + 1)],
                    xn_h[:, 512 * tg:512 * (tg + 1)], start=True, stop=True),
                    reads=[("xnT", hb, tg), ("wdkvc", h)], writes=[f"pb{k}"])
                P.op("dve", lambda v, lc=lc, tg=tg, k=k: v.tensor_copy(
                    out=cT[:, lc, 512 * tg:512 * (tg + 1)], in_=pb[k][:]),
                    reads=[f"pb{k}"], writes=[("cT", lc, tg)])

        def vt_mm(t4, h=h):
            k = 2 + (t4 % 2)
            for sub in range(4):
                tc = 4 * t4 + sub
                for lc in range(4):
                    P.op("pe", lambda t, tc=tc, lc=lc, sub=sub, k=k, h=h: t.matmul(
                        pb[k][:, 128 * sub:128 * (sub + 1)],
                        cT[:, lc, 128 * tc:128 * (tc + 1)],
                        wuv2g[:, h, 128 * lc:128 * (lc + 1)],
                        start=(lc == 0), stop=(lc == 3)),
                        reads=[("cT", lc, tc // 4), ("wuv2g", h)], writes=[f"pb{k}"])
            P.op("dve", lambda v, t4=t4, k=k: v.tensor_copy(
                out=vt[:, 4 * t4:4 * (t4 + 1), :].rearrange("p a b -> p (a b)"),
                in_=pb[k][:]),
                reads=[f"pb{k}"], writes=[("vt", t4)])

        NG = S // 512
        for tg in range(NG + 1):
            if tg < NG:
                for i4 in range(4):
                    stats_mm(4 * tg + i4)
                ct_mm(tg)
            if tg >= 1:
                vt_mm(tg - 1)
        P.op("act", lambda a: a.activation(out=lnv, in_=ssq, func=AF.Ln,
                                           bias=cc[:, 0:1]),
             reads=["ssq", "cc"], writes=["lnv"])
        P.op("act", lambda a: a.activation(out=As_, in_=lnv, func=AF.Exp,
                                           scale=cc[:, 1:2], bias=cc[:, 3:4]),
             reads=["lnv", "cc"], writes=["As"])
        P.op("act", lambda a: a.mul(out=lnA, in_=lnv, mul=cc[:, 1:2]),
             reads=["lnv", "cc"], writes=["lnA"])
        P.op("act", lambda a: a.activation(out=invA, in_=lnv, func=AF.Exp,
                                           scale=cc[:, 2:3]),
             reads=["lnv", "cc"], writes=["invA"])
        if h + 1 < H:
            load_xnT(h + 1)

        if deferred[0] is not None:
            deferred[0]()
            deferred[0] = None

        for j in range(NJ):
            nk = 16 * (j + 1)
            for lc in range(4):
                k = 2 + (lc % 2)
                P.op("pe", lambda t, lc=lc, j=j, k=k, h=h: t.matmul(
                    pb[k][:], absorbed[:, h, 128 * lc:128 * (lc + 1)],
                    xqT[hb][:, 512 * j:512 * (j + 1)], start=True, stop=True),
                    reads=[("xqT", hb), ("abs", h)], writes=[f"pb{k}"])
                P.op("dve", lambda v, lc=lc, k=k: v.tensor_copy(
                    out=qlat[:, lc, :], in_=pb[k][:]),
                    reads=[f"pb{k}"], writes=[("qlat", lc)])
            if j == NJ - 1 and h + 1 < H:
                load_xqT(h + 1)
            pend = []
            mcnt = [0]
            SB = (0, 1, 6, 7)

            fl_n = [0]

            def flush(P=P, pend=pend, nk=nk, fl_n=fl_n):
                kk, wsel, wkey = pend.pop(0)
                fl = fl_n[0]
                fl_n[0] += 1
                ra = fl % 2
                if fl < 2:
                    P.op("dve", lambda v, kk=kk, w=wsel, ra=ra: v.tensor_scalar(
                        out=racc[ra], in0=w, scalar1=invA[:, kk:kk + 1],
                        scalar2=None, op0=OP.mult),
                        reads=[wkey, "invA"], writes=[("racc", ra)])
                else:
                    P.op("dve", lambda v, kk=kk, w=wsel, ra=ra: v.scalar_tensor_tensor(
                        out=racc[ra], in0=w, scalar=invA[:, kk:kk + 1],
                        in1=racc[ra], op0=OP.mult, op1=OP.add),
                        reads=[wkey, "invA", ("racc", ra)], writes=[("racc", ra)])
                P.op("pe", lambda t, kk=kk, w=wsel, fl=fl: t.matmul(
                    pb[5][:], vt[:, kk, :], w,
                    start=(fl == 0), stop=(fl == nk - 1)),
                    reads=[wkey, ("vt", kk // 4)], writes=["pb5"])

            full = list(range(16 * j))
            win = list(range(16 * j, nk))
            order = []
            fi = wi = 0
            while fi < len(full) or wi < len(win):
                take = max(1, len(full) // len(win))
                for _ in range(take):
                    if fi < len(full):
                        order.append(full[fi]); fi += 1
                if wi < len(win):
                    order.append(win[wi]); wi += 1
            nld = [0]

            def mload(P=P, nld=nld, j=j):
                w = 16 * j + nld[0]
                r = nld[0] % 6
                P.op("gp", lambda g, w=w, r=r: g.dma_start(
                    out=mring[r], in_=mask_hbm[w]),
                    reads=[("mhbm", w)], writes=[("mring", r)], dma=True)
                nld[0] += 1

            for _ in range(6):
                mload()
            for ci, kk in enumerate(order):
                k = SB[ci % 4]
                kw = ci % 5
                for lc in range(4):
                    P.op("pe", lambda t, kk=kk, lc=lc, k=k: t.matmul(
                        pb[k][:], cT[:, lc, 128 * kk:128 * (kk + 1)], qlat[:, lc, :],
                        start=(lc == 0), stop=(lc == 3)),
                        reads=[("cT", lc, kk // 4), ("qlat", lc)], writes=[f"pb{k}"])
                P.op("act", lambda a, kk=kk, k=k, kw=kw: a.activation(
                    out=w2[kw], in_=pb[k][:], func=AF.Exp,
                    scale=As_[:, kk:kk + 1], bias=lnA[:, kk:kk + 1]),
                    reads=[f"pb{k}", "As", "lnA"], writes=[("w2", kw)])
                if ci == 4 and deferred[0] is not None:
                    deferred[0]()
                    deferred[0] = None
                while len(pend) >= 4:
                    flush()
                if kk >= 16 * j:
                    mi = mcnt[0] % 4
                    mr = (kk - 16 * j) % 6
                    mcnt[0] += 1
                    P.op("dve", lambda v, kw=kw, mi=mi, mr=mr: v.tensor_mul(
                        out=w2m[mi], in0=w2[kw], in1=mring[mr]),
                        reads=[("w2", kw), ("mring", mr)], writes=[("w2m", mi)])
                    if nld[0] < 16:
                        mload()
                    pend.append((kk, w2m[mi], ("w2m", mi)))
                else:
                    pend.append((kk, w2[kw], ("w2", kw)))
            while pend:
                flush()

            P.op("dve", lambda v: v.tensor_add(out=racc[0], in0=racc[0],
                                               in1=racc[1]),
                 reads=[("racc", 0), ("racc", 1)], writes=[("racc", 0)])
            P.op("pe", lambda t: t.matmul(pb[4][0:1, :], cc[:, 6:7], racc[0],
                                          start=True, stop=True),
                 reads=[("racc", 0), "cc"], writes=["pb4"])
            def epilogue(P=P, h=h, j=j):
                i = j % 2
                P.op("act", lambda a: a.activation(out=rs_row, in_=pb[4][0:1, :],
                                                   func=AF.Ln),
                     reads=["pb4"], writes=["rs_row"])
                P.op("act", lambda a: a.activation(out=rowb, in_=rs_row, func=AF.Exp,
                                                   scale=cc[0:1, 5:6]),
                     reads=["rs_row", "cc"], writes=["rowb"])
                P.op("pe", lambda t: t.matmul(pb[4][:], ones_sb, rowb,
                                              start=True, stop=True),
                     reads=["rowb", "ones"], writes=["pb4"])
                P.op("dve", lambda v: v.tensor_copy(out=rb_sb, in_=pb[4][:]),
                     reads=["pb4"], writes=["rb_sb"])
                P.op("dve", lambda v: v.tensor_mul(out=o_tmp, in0=pb[5][:], in1=rb_sb),
                     reads=["pb5", "rb_sb"], writes=["o_tmp"])
                P.op("act", lambda a: a.activation(
                    out=oh2stg[i], in_=o_tmp, func=AF.Identity, bias=vbeta[:, h:h + 1]),
                    reads=["o_tmp", ("vbeta", h)], writes=[("oh2stg", i)])
                P.op("gp", lambda g: g.dma_start(
                    out=oh2_hbm[128 * h:128 * (h + 1), 512 * j:512 * (j + 1)],
                    in_=oh2stg[i]),
                    reads=[("oh2stg", i)], writes=[("oh2", h, j)], dma=True)

            deferred[0] = epilogue

    if deferred[0] is not None:
        deferred[0]()
        deferred[0] = None

    P.barrier(nops)

    for dc in range(16):
        P.op("sync", lambda s, dc=dc: s.dma_start_transpose(
            xnqT[:, dc, :], xnq_hbm[:, 128 * dc:128 * (dc + 1)]),
            reads=all_xnq, writes=[("xnqT", dc)], dma=True)
    P.op("gp", lambda g: g.dma_start(
        out=gwb, in_=gate_w.rearrange("(k p) e -> p k e", p=128)),
        writes=["gwb"], dma=True)
    P.op("gp", lambda g: g.dma_start(
        out=gbrow, in_=gate_b.rearrange("(a d) -> a d", a=1)),
         writes=["gbrow"], dma=True)
    for tt in range(16):
        gk = 6 + (tt % 2)
        for dc in range(16):
            P.op("pe", lambda t, tt=tt, dc=dc, gk=gk: t.matmul(
                pb[gk][:, 0:8], xnqT[:, dc, 128 * tt:128 * (tt + 1)], gwb[:, dc, :],
                start=(dc == 0), stop=False),
                reads=[("xnqT", dc), "gwb"], writes=[f"pb{gk}"])
        P.op("pe", lambda t, gk=gk: t.matmul(pb[gk][:, 0:8], ones_sb, gbrow,
                                             start=False, stop=True),
             reads=["gbrow", "ones"], writes=[f"pb{gk}"])
        P.op("dve", lambda v, gk=gk: v.tensor_copy(out=g8, in_=pb[gk][:, 0:8]),
             reads=[f"pb{gk}"], writes=["g8"])
        P.op("dve", lambda v: v.max(out=mx8, in_=g8), reads=["g8"], writes=["mx8"])
        P.op("dve", lambda v, tt=tt: v.tensor_scalar(
            out=mask_sb[:, tt, :], in0=g8, scalar1=mx8[:, 1:2], scalar2=0.5,
            op0=OP.is_ge, op1=OP.mult),
            reads=["g8", "mx8"], writes=[("mask", tt)])

    for oc in range(4):
        ocs = slice(512 * oc, 512 * (oc + 1))
        for e in range(8):
            i = e % 2
            P.op("gp", lambda g, e=e, oc=oc, i=i: g.dma_start(
                out=exsl[i],
                in_=expert_w[e, :, 512 * oc:512 * (oc + 1)].rearrange(
                    "(k p) o -> p k o", p=128)),
                writes=[("ex", i)], dma=True)
            P.op("gp", lambda g, e=e, oc=oc, i=i: g.dma_start(
                out=berow[i], in_=expert_b[e:e + 1, 512 * oc:512 * (oc + 1)]),
                writes=[("be", i)], dma=True)
            for tt in range(16):
                k = tt % 2
                for dc in range(16):
                    P.op("pe", lambda t, tt=tt, dc=dc, k=k, i=i: t.matmul(
                        pb[k][:], xnqT[:, dc, 128 * tt:128 * (tt + 1)], exsl[i][:, dc, :],
                        start=(dc == 0), stop=False),
                        reads=[("xnqT", dc), ("ex", i)], writes=[f"pb{k}"])
                P.op("pe", lambda t, k=k, i=i: t.matmul(
                    pb[k][:], ones_sb, berow[i], start=False, stop=True),
                    reads=[("be", i), "ones"], writes=[f"pb{k}"])
                if e == 0:
                    P.op("act", lambda a, tt=tt, k=k: a.activation(
                        out=acc[:, tt, :], in_=pb[k][:], func=AF.Relu,
                        scale=mask_sb[:, tt, 0:1]),
                        reads=[f"pb{k}", ("mask", tt)], writes=[("acc", tt)])
                else:
                    r = tt % 2
                    P.op("act", lambda a, tt=tt, k=k, e=e, r=r: a.activation(
                        out=relutmp[r], in_=pb[k][:], func=AF.Relu,
                        scale=mask_sb[:, tt, e:e + 1]),
                        reads=[f"pb{k}", ("mask", tt)], writes=[("rlt", r)])
                    P.op("dve", lambda v, tt=tt, r=r: v.tensor_add(
                        out=acc[:, tt, :], in0=acc[:, tt, :], in1=relutmp[r]),
                        reads=[("acc", tt), ("rlt", r)], writes=[("acc", tt)])
        for hc in range(16):
            P.op("sync", lambda s, hc=hc, oc=oc: s.dma_start_transpose(
                wot[:, hc, :], wo_bf[512 * oc:512 * (oc + 1), 128 * hc:128 * (hc + 1)]),
                reads=[("bf", "wo")], writes=[("wot", hc)], dma=True)
        P.op("gp", lambda g, ocs=ocs: g.dma_start(
            out=borow, in_=bo.rearrange("(a d) -> a d", a=1)[:, ocs]),
             writes=["borow"], dma=True)
        for j in range(NJ):
            for hc in range(16):
                P.op("sync", lambda s, hc=hc, j=j: s.dma_start(
                    out=oh2sb[:, hc, :],
                    in_=oh2_hbm[128 * hc:128 * (hc + 1), 512 * j:512 * (j + 1)]),
                    reads=[("oh2", hc, j)], writes=[("oh2sb", hc, j)], dma=True)
            for ts_ in range(4):
                tt = 4 * j + ts_
                k = 2 + (tt % 2)
                for hc in range(16):
                    P.op("pe", lambda t, hc=hc, ts_=ts_, k=k, j=j: t.matmul(
                        pb[k][:], oh2sb[:, hc, 128 * ts_:128 * (ts_ + 1)], wot[:, hc, :],
                        start=(hc == 0), stop=False),
                        reads=[("oh2sb", hc, j), ("wot", hc)], writes=[f"pb{k}"])
                P.op("pe", lambda t, k=k: t.matmul(
                    pb[k][:], ones_sb, borow, start=False, stop=True),
                    reads=["borow", "ones"], writes=[f"pb{k}"])
                i = tt % 2
                P.op("sync", lambda s, tt=tt, ocs=ocs, i=i: s.dma_start(
                    out=xres[i], in_=x_own[128 * tt:128 * (tt + 1), ocs]),
                    writes=[("xres", i)], dma=True)
                P.op("dve", lambda v, tt=tt, k=k, i=i: v.tensor_add(
                    out=relutmp[i], in0=pb[k][:], in1=acc[:, tt, :]),
                    reads=[f"pb{k}", ("acc", tt)], writes=[("rlt", i)])
                P.op("dve", lambda v, tt=tt, i=i: v.tensor_add(
                    out=outstg[i], in0=relutmp[i], in1=xres[i]),
                    reads=[("rlt", i), ("xres", i)], writes=[("outstg", i)])
                P.op("gp", lambda g, tt=tt, ocs=ocs, i=i: g.dma_start(
                    out=out_own[128 * tt:128 * (tt + 1), ocs], in_=outstg[i]),
                    reads=[("outstg", i)], writes=[("out", tt, oc)], dma=True)

    P.op("gp", None, reads=[("out", tt, oc) for tt in range(16) for oc in range(4)])

    P.finalize()

    def run_engine(handle, eng):
        for e in P.ev:
            if e["eng"] != eng:
                continue
            for peng, val in e["wl"]:
                handle.wait_ge(sem[peng], val)
            if e["fn"] is not None:
                ins = e["fn"](handle)
                if e["inc"]:
                    ins.then_inc(sem[eng], e["inc"])
            else:
                assert e["inc"] == 0

    with es:
        @block.tensor
        def _(t):
            run_engine(t, "pe")

        @block.scalar
        def _(a):
            run_engine(a, "act")

        @block.vector
        def _(v):
            run_engine(v, "dve")

        @block.sync
        def _(s):
            run_engine(s, "sync")

        @block.gpsimd
        def _(g):
            run_engine(g, "gp")

    return nc


_NC_CACHE = None


def kernel(L0_attn_x, L0_attn_Wq, L0_attn_Wdkv, L0_attn_Wuk, L0_attn_Wuv,
           L0_attn_ln_g, L0_attn_ln_b, L0_attn_Wo_h, L0_attn_Wo, L0_attn_bo,
           L0_ffn_gate_w, L0_ffn_gate_b, L0_ffn_expert_w, L0_ffn_expert_b):
    global _NC_CACHE
    x = np.ascontiguousarray(L0_attn_x, np.float32)
    iota_q = np.broadcast_to(np.arange(512, dtype=np.float16), (128, 512)).copy()
    ones_r = np.ones((1, 128), dtype=BF)
    lngcol = np.ascontiguousarray(
        np.asarray(L0_attn_ln_g, np.float32).reshape(16, 4, 128).transpose(2, 0, 1)
        .reshape(128, 64))
    lnbcol = np.ascontiguousarray(
        np.asarray(L0_attn_ln_b, np.float32).reshape(16, 4, 128).transpose(2, 0, 1)
        .reshape(128, 64))
    shared = {
        "Wq": np.ascontiguousarray(L0_attn_Wq, np.float32),
        "Wdkv": np.ascontiguousarray(L0_attn_Wdkv, np.float32),
        "Wuk": np.ascontiguousarray(L0_attn_Wuk, np.float32),
        "Wuv": np.ascontiguousarray(L0_attn_Wuv, np.float32),
        "Wo_h": np.ascontiguousarray(L0_attn_Wo_h, np.float32),
        "Wo": np.ascontiguousarray(L0_attn_Wo, np.float32),
        "bo": np.ascontiguousarray(L0_attn_bo, np.float32),
        "gate_w": np.ascontiguousarray(L0_ffn_gate_w, np.float32),
        "gate_b": np.ascontiguousarray(L0_ffn_gate_b, np.float32),
        "expert_w": np.ascontiguousarray(L0_ffn_expert_w, np.float32),
        "expert_b": np.ascontiguousarray(L0_ffn_expert_b, np.float32),
        "lngcol": lngcol, "lnbcol": lnbcol,
        "iota_q": iota_q, "ones_r": ones_r,
        "consts": np.tile(np.array([[EPS, -0.5, 0.5, LN_ISQ, 1.0 / L,
                                     -1.0, 1.0, ISQ]], np.float32), (128, 1)),
    }
    in_maps = []
    groups_all = []
    for core in range(8):
        b, ii = core // 4, core % 4
        groups = [ii, 7 - ii, 8 + ii, 15 - ii]
        groups_all.append((b, groups))
        x_own = np.ascontiguousarray(
            np.concatenate([x[b, 512 * g:512 * (g + 1)] for g in groups], axis=0))
        sm = np.empty((128, 64), np.float16)
        p = np.arange(128)
        for j in range(4):
            q0 = 512 * groups[j]
            for kk in range(16):
                t0 = 128 * (16 * j + kk)
                sm[:, 16 * j + kk] = (t0 + p - q0).astype(np.float16)
        im = dict(shared)
        im["x_full"] = x[b]
        im["x_own"] = x_own
        im["smask"] = sm
        in_maps.append(im)

    if _NC_CACHE is None:
        _NC_CACHE = build_nc()
    res = run_bass_kernel_spmd(_NC_CACHE, in_maps, core_ids=list(range(8)))

    out = np.empty((2, S, D), np.float32)
    for core in range(8):
        b, groups = groups_all[core]
        r = res.results[core]["out_own"]
        for j, g in enumerate(groups):
            out[b, 512 * g:512 * (g + 1)] = r[512 * j:512 * (j + 1)]
    return out


# version 17
# speedup vs baseline: 1.2097x; 1.2097x over previous
import contextlib
import numpy as np
import ml_dtypes
import concourse.bass as bass
import concourse.mybir as mybir
from concourse.bass_utils import run_bass_kernel_spmd

F32 = mybir.dt.float32
BF16 = mybir.dt.bfloat16
FP16 = mybir.dt.float16
U8 = mybir.dt.uint8
AF = mybir.ActivationFunctionType
OP = mybir.AluOpType
AX = mybir.AxisListType.X
BF = ml_dtypes.bfloat16

S, D, H, L, HD = 8192, 2048, 16, 512, 128
NJ = 4
OWN = 2048
NT = S // 128
EPS = 1e-5
ISQ = float(1.0 / np.sqrt(np.float64(L)))
LN_ISQ = float(np.log(1.0 / np.sqrt(np.float64(L))))

ENGS = ("pe", "act", "dve", "sync", "gp")


class Plan:

    def __init__(self):
        self.ev = []
        self.last_w = {}
        self.last_r = {}

    def op(self, eng, fn, reads=(), writes=(), dma=False):
        i = len(self.ev)
        deps = set()
        for k in reads:
            if k in self.last_w:
                deps.add(self.last_w[k])
        for k in writes:
            if k in self.last_w:
                deps.add(self.last_w[k])
            for j in self.last_r.get(k, {}).values():
                deps.add(j)
        self.ev.append(dict(i=i, eng=eng, fn=fn, deps=deps, dma=dma))
        for k in reads:
            self.last_r.setdefault(k, {})[eng] = i
        for k in writes:
            self.last_w[k] = i
            self.last_r[k] = {}
        return i

    def barrier(self, nops):
        bid = len(self.ev)
        pending = list(self.last_w.keys())
        for e in ENGS:
            self.op(e, nops[e], reads=pending, writes=[("bar", bid, e)])
        for e in ENGS:
            self.op(e, None, reads=[("bar", bid, e2) for e2 in ENGS if e2 != e])

    def finalize(self):
        need = [False] * len(self.ev)
        for e in self.ev:
            byeng = {}
            for d in sorted(e["deps"]):
                p = self.ev[d]
                if p["eng"] == e["eng"] == "pe":
                    continue
                if byeng.get(p["eng"], -1) < d:
                    byeng[p["eng"]] = d
            e["waits"] = byeng
            for d in byeng.values():
                need[d] = True
        cnt = {g: 0 for g in ENGS}
        for e in self.ev:
            if e["dma"]:
                need[e["i"]] = True
            if need[e["i"]]:
                e["inc"] = 16 if e["dma"] else 1
                cnt[e["eng"]] += e["inc"]
            else:
                e["inc"] = 0
            e["cum"] = cnt[e["eng"]]
        water = {}
        for e in self.ev:
            wl = []
            for peng in sorted(e["waits"]):
                val = self.ev[e["waits"][peng]]["cum"]
                assert self.ev[e["waits"][peng]]["inc"] > 0
                key = (e["eng"], peng)
                if water.get(key, -1) >= val:
                    continue
                water[key] = val
                wl.append((peng, val))
            e["wl"] = wl
        self.totals = cnt


def build_nc():
    nc = bass.Bass("TRN2", target_bir_lowering=False, debug=False)

    def din(name, shape, dt=F32):
        return nc.dram_tensor(name, shape, dt, kind="ExternalInput").ap()

    x_full = din("x_full", [S, D])
    x_own = din("x_own", [OWN, D])
    Wq = din("Wq", [H, HD, HD])
    Wdkv = din("Wdkv", [H, L, HD])
    Wuk = din("Wuk", [H, L, HD])
    Wuv = din("Wuv", [H, L, HD])
    Wo_h = din("Wo_h", [H, HD, HD])
    Wo = din("Wo", [D, D])
    bo = din("bo", [D])
    gate_w = din("gate_w", [D, 8])
    gate_b = din("gate_b", [8])
    expert_w = din("expert_w", [8, D, D])
    expert_b = din("expert_b", [8, D])
    lngcol = din("lngcol", [128, 64])
    lnbcol = din("lnbcol", [128, 64])
    iota_q = din("iota_q", [128, 512], FP16)
    smask = din("smask", [128, 64], FP16)
    ones_r = din("ones_r", [1, 128], BF16)
    consts = din("consts", [128, 8])

    out_own = nc.dram_tensor("out_own", [OWN, D], F32, kind="ExternalOutput").ap()

    def dint(name, shape, dt=BF16):
        return nc.dram_tensor(name, shape, dt, kind="Internal").ap()

    xn_hbm = dint("xn_hbm", [S, D])
    xnq_hbm = dint("xnq_hbm", [OWN, D])
    oh2_hbm = dint("oh2_hbm", [D, OWN])
    mask_hbm = dint("mask_hbm", [64, 128, 512])
    wq_bf = dint("wq_bf", [H, HD, HD])
    wdkv_bf = dint("wdkv_bf", [H, L, HD])
    wukp_bf = dint("wukp_bf", [H, L, HD])
    wuv_bf = dint("wuv_bf", [H, L, HD])
    woh_bf = dint("woh_bf", [H, HD, HD])
    wo_bf = dint("wo_bf", [D, D])

    ARENA_BYTES = 210752
    backing = nc.alloc_sbuf_tensor("arena", [128, ARENA_BYTES], U8)
    base = nc.lookup_mloc(backing).addr
    cur = [0]

    def place(name, shape, dt, off=None):
        nbytes = int(np.prod(shape[1:])) * mybir.dt.size(dt)
        if off is None:
            off = cur[0]
            cur[0] = (off + nbytes + 31) & ~31
        else:
            assert off + nbytes <= ARENA_BYTES, name
        assert off + nbytes <= ARENA_BYTES, (name, off, nbytes)
        h = nc.virtual_sbuf_tensor(name, list(shape), dt)
        nc._tensor(h.name, h.shape, h.dtype, type="SB")
        mloc = nc.lookup_mloc(h)
        mloc.allocated = True
        mloc.addr = base + off
        return h.ap()

    A1, A2, A3, A4, A5, A6 = 0, 65536, 98304, 114688, 131072, 147456
    ST = 163840
    cur[0] = ST
    x_t = [place(f"x_t{n}", [128, D], F32, A1 + 8192 * n) for n in range(4)]
    xnout = [place(f"xnout{n}", [128, D], BF16, A1 + 32768 + 4096 * n)
             for n in range(4)]
    cT = place("cT", [128, 4, S], BF16, A1)
    xnT = [place("xnT0", [128, S], BF16, A2), place("xnT1", [128, S], BF16, A2 + 16384)]
    vt = place("vt", [128, NT, 128], BF16, A3)
    wdkvc = place("wdkvc", [128, H, L], BF16, A4)
    absorbed = place("absorbed", [128, H, L], BF16, A5)
    wuv2g = place("wuv2g", [128, H, L], BF16, A6)
    o = A3
    wukstg = [place("wukstg0", [128, 4, 128], F32, o),
              place("wukstg1", [128, 4, 128], F32, o + 2048)]
    wukpstg = [place("wukpstg0", [128, 4, 128], BF16, o + 4096),
               place("wukpstg1", [128, 4, 128], BF16, o + 5120)]
    wqT = place("wqT", [128, 128], BF16, o + 6144)
    wukpT = place("wukpT", [128, 512], BF16, o + 6400)
    wdkvT = place("wdkvT", [128, 512], BF16, o + 7424)
    wuvT = place("wuvT", [128, 512], BF16, o + 8448)
    wohT = place("wohT", [128, 128], BF16, o + 9472)
    wuv2raw = place("wuv2raw", [128, 512], BF16, o + 9728)
    msum = place("msum", [128, 1], F32, o + 10752)
    mcol = place("mcol", [128, 1], F32, o + 10784)
    xnqT = place("xnqT", [128, 16, D], BF16, A1)
    acc = place("acc", [128, 16, 512], F32, A2)
    oh2sb = place("oh2sb", [128, 16, 512], BF16, A3)
    wot = place("wot", [128, 16, 512], BF16, A4)
    exsl = [place("ex0", [128, 16, 512], BF16, A5), place("ex1", [128, 16, 512], BF16, A6)]

    xqT = [place("xqT0", [128, OWN], BF16), place("xqT1", [128, OWN], BF16)]
    ones_sb = place("ones_sb", [1, 128], BF16)
    cc = place("cc_sb", [128, 8], F32)
    lng_sb = place("lng_sb", [128, 64], F32)
    lnb_sb = place("lnb_sb", [128, 64], F32)
    lnbb_sb = place("lnbb_sb", [128, 64], BF16)
    vbeta = place("vbeta", [128, H], F32)
    iota_sb = place("iota_sb", [128, 512], FP16)
    smask_sb = place("smask_sb", [128, 64], FP16)
    smaskf = place("smaskf", [128, 64], F32)
    ph2s = cur[0]
    w2 = [place(f"w2_{n}", [128, 512], BF16) for n in range(5)]
    w2m = [place(f"w2m{n}", [128, 512], BF16) for n in range(4)]
    qlat = place("qlat", [128, 4, 512], BF16)
    ssq = place("ssq", [128, NT], F32)
    lnv = place("lnv", [128, NT], F32)
    As_ = place("As_", [128, NT], F32)
    lnA = place("lnA", [128, NT], F32)
    invA = place("invA", [128, NT], F32)
    sqdump = place("sqdump", [128, 512], BF16)
    bn6 = place("bn6", [128, 6], F32)
    bn2 = place("bn2", [128, 2], F32)
    rs_row = place("rs_row", [1, 512], F32)
    rowb = place("rowb", [1, 512], BF16)
    rb_sb = place("rb_sb", [128, 512], F32)
    racc = [place("racc0", [128, 512], F32), place("racc1", [128, 512], F32)]
    mring = [place(f"mring{n}", [128, 512], BF16) for n in range(6)]
    o_tmp = place("o_tmp", [128, 512], F32)
    oh2stg = [place("oh2stg0", [128, 512], BF16), place("oh2stg1", [128, 512], BF16)]
    st24 = [place(f"st24_{n}", [128, 24], F32) for n in range(4)]
    st2 = [place(f"st2_{n}", [128, 2], F32) for n in range(4)]
    nm1 = [place(f"nm1_{n}", [128, 1], F32) for n in range(4)]
    lnv1 = [place(f"lnv1_{n}", [128, 1], F32) for n in range(4)]
    rs1 = [place(f"rs1_{n}", [128, 1], F32) for n in range(4)]
    nmrs = [place(f"nmrs_{n}", [128, 1], F32) for n in range(4)]
    ph2e = cur[0]
    assert ph2e <= ARENA_BYTES, ph2e
    cur[0] = ph2s
    g8 = place("g8", [128, 8], F32)
    mx8 = place("mx8", [128, 8], F32)
    mask_sb = place("mask_sb", [128, 16, 8], F32)
    gwb = place("gwb", [128, 16, 8], BF16)
    gbrow = place("gbrow", [1, 8], BF16)
    borow = place("borow", [1, 512], BF16)
    berow = [place("berow0", [1, 512], BF16), place("berow1", [1, 512], BF16)]
    xres = [place("xres0", [128, 512], F32), place("xres1", [128, 512], F32)]
    outstg = [place("outstg0", [128, 512], F32), place("outstg1", [128, 512], F32)]
    relutmp = [place("rlt0", [128, 512], F32), place("rlt1", [128, 512], F32)]
    assert cur[0] <= ph2e, (cur[0], ph2e)

    pb = [nc.alloc_psum_tensor(f"pb{k}", [128, 512], F32).ap() for k in range(8)]

    es = contextlib.ExitStack()
    sem = {g: es.enter_context(nc.semaphore(f"sem_{g}")) for g in ENGS}
    block = es.enter_context(nc.Block())

    P = Plan()
    KW = dict()

    for nm, src, dst in (("wq", Wq, wq_bf), ("wdkv", Wdkv, wdkv_bf),
                         ("wuv", Wuv, wuv_bf), ("woh", Wo_h, woh_bf),
                         ("wo", Wo, wo_bf)):
        P.op("gp", lambda g, s=src, d=dst: g.dma_start(out=d, in_=s),
             writes=[("bf", nm)], dma=True)
    P.op("sync", lambda s: s.dma_start(out=iota_sb, in_=iota_q), writes=["iota"], dma=True)
    P.op("sync", lambda s: s.dma_start(out=smask_sb, in_=smask), writes=["smask"], dma=True)
    P.op("sync", lambda s: s.dma_start(out=lng_sb, in_=lngcol), writes=["lng"], dma=True)
    P.op("sync", lambda s: s.dma_start(out=lnb_sb, in_=lnbcol), writes=["lnb"], dma=True)
    P.op("sync", lambda s: s.dma_start(out=ones_sb, in_=ones_r), writes=["ones"], dma=True)
    P.op("sync", lambda s: s.dma_start(out=cc, in_=consts), writes=["cc"], dma=True)
    P.op("dve", lambda v: v.tensor_copy(out=lnbb_sb, in_=lnb_sb),
         reads=["lnb"], writes=["lnbb"])
    def prep_wdkvc():
        for h in range(H):
            P.op("sync", lambda s, h=h: s.dma_start_transpose(wdkvT, wdkv_bf[h]),
                 reads=[("bf", "wdkv")], writes=["wdkvT"], dma=True)
            P.op("dve", lambda v: v.reduce_sum(out=msum, in_=wdkvT, axis=AX),
                 reads=["wdkvT"], writes=["msum"])
            P.op("dve", lambda v: v.tensor_scalar(out=mcol, in0=msum, scalar1=-1.0 / L,
                                                  scalar2=None, op0=OP.mult),
                 reads=["msum"], writes=["mcol"])
            P.op("dve", lambda v, h=h: v.tensor_scalar(
                out=wdkvc[:, h, :], in0=wdkvT, scalar1=mcol, scalar2=None, op0=OP.add),
                reads=["wdkvT", "mcol"], writes=[("wdkvc", h)])

    def prep_weights():
      for h in range(H):
        i = h % 2
        P.op("gp", lambda g, h=h, i=i: g.dma_start(
            out=wukstg[i], in_=Wuk[h].rearrange("(lc p) j -> p lc j", p=128)),
            writes=[("wukstg", i)], dma=True)
        for lc in range(4):
            P.op("dve", lambda v, h=h, lc=lc, i=i: v.tensor_scalar(
                out=wukpstg[i][:, lc, :], in0=wukstg[i][:, lc, :],
                scalar1=lng_sb[:, h * 4 + lc:h * 4 + lc + 1],
                scalar2=None, op0=OP.mult),
                reads=[("wukstg", i), "lng"], writes=[("wukpstg", i)])
        P.op("gp", lambda g, h=h, i=i: g.dma_start(
            out=wukp_bf[h].rearrange("(lc p) j -> p lc j", p=128), in_=wukpstg[i]),
            reads=[("wukpstg", i)], writes=[("wukp", h)], dma=True)
      for h in range(H):
        P.op("sync", lambda s, h=h: s.dma_start_transpose(wqT, wq_bf[h]),
             reads=[("bf", "wq")], writes=["wqT"], dma=True)
        P.op("sync", lambda s, h=h: s.dma_start_transpose(wukpT, wukp_bf[h]),
             reads=[("wukp", h)], writes=["wukpT"], dma=True)
        P.op("pe", lambda t: t.matmul(pb[4][:], wqT, wukpT, start=True, stop=True),
             reads=["wqT", "wukpT"], writes=["pb4"])
        P.op("dve", lambda v, h=h: v.tensor_copy(out=absorbed[:, h, :], in_=pb[4][:]),
             reads=["pb4"], writes=[("abs", h)])
        P.op("sync", lambda s, h=h: s.dma_start_transpose(wuvT, wuv_bf[h]),
             reads=[("bf", "wuv")], writes=["wuvT"], dma=True)
        P.op("sync", lambda s, h=h: s.dma_start_transpose(wohT, woh_bf[h]),
             reads=[("bf", "woh")], writes=["wohT"], dma=True)
        for lc in range(4):
            P.op("pe", lambda t, lc=lc: t.matmul(
                pb[5][:, 128 * lc:128 * (lc + 1)], wuvT[:, 128 * lc:128 * (lc + 1)],
                wohT, start=True, stop=True),
                reads=["wuvT", "wohT"], writes=["pb5"])
        P.op("dve", lambda v: v.tensor_copy(out=wuv2raw, in_=pb[5][:]),
             reads=["pb5"], writes=["wuv2raw"])
        for lc in range(4):
            P.op("dve", lambda v, h=h, lc=lc: v.tensor_scalar(
                out=wuv2g[:, h, 128 * lc:128 * (lc + 1)],
                in0=wuv2raw[:, 128 * lc:128 * (lc + 1)],
                scalar1=lng_sb[:, h * 4 + lc:h * 4 + lc + 1], scalar2=None, op0=OP.mult),
                reads=["wuv2raw", "lng"], writes=[("wuv2g", h)])
        for lc in range(4):
            P.op("pe", lambda t, h=h, lc=lc: t.matmul(
                pb[4][:, 0:1], wuv2raw[:, 128 * lc:128 * (lc + 1)],
                lnbb_sb[:, h * 4 + lc:h * 4 + lc + 1],
                start=(lc == 0), stop=(lc == 3)),
                reads=["wuv2raw", "lnbb"], writes=["pb4"])
        P.op("dve", lambda v, h=h: v.tensor_copy(out=vbeta[:, h:h + 1], in_=pb[4][:, 0:1]),
             reads=["pb4"], writes=[("vbeta", h)])

    prep_wdkvc()

    def ln_block(blk, src_rows, dst, key):
        i = blk % 4
        P.op("sync", lambda s, r=src_rows, i=i: s.dma_start(out=x_t[i], in_=r),
             writes=[("x", i)], dma=True)
        for q in range(4):
            P.op("dve", lambda v, i=i, q=q: v.bn_stats(
                out=st24[i][:, 6 * q:6 * (q + 1)], in_=x_t[i][:, 512 * q:512 * (q + 1)]),
                reads=[("x", i)], writes=[("st24", i)])
        P.op("dve", lambda v, i=i: v.bn_aggr(out=st2[i], in_=st24[i]),
             reads=[("st24", i)], writes=[("st2", i)])
        P.op("dve", lambda v, i=i: v.tensor_scalar(out=nm1[i], in0=st2[i][:, 0:1],
                                                   scalar1=-1.0, scalar2=None, op0=OP.mult),
             reads=[("st2", i)], writes=[("nm1", i)])
        P.op("act", lambda a, i=i: a.activation(out=lnv1[i], in_=st2[i][:, 1:2], func=AF.Ln,
                                                bias=cc[:, 0:1]),
             reads=[("st2", i), "cc"], writes=[("lnv1", i)])
        P.op("act", lambda a, i=i: a.activation(out=rs1[i], in_=lnv1[i], func=AF.Exp,
                                                scale=cc[:, 1:2]),
             reads=[("lnv1", i), "cc"], writes=[("rs1", i)])
        P.op("act", lambda a, i=i: a.mul(out=nmrs[i], in_=rs1[i], mul=nm1[i]),
             reads=[("rs1", i), ("nm1", i)], writes=[("nmrs", i)])
        P.op("act", lambda a, i=i: a.activation(out=xnout[i], in_=x_t[i], func=AF.Identity,
                                                scale=rs1[i], bias=nmrs[i]),
             reads=[("x", i), ("rs1", i), ("nmrs", i)], writes=[("xnout", i)])
        P.op("gp", lambda g, d=dst, i=i: g.dma_start(out=d, in_=xnout[i]),
             reads=[("xnout", i)], writes=[key], dma=True)

    all_xnq = [("xnq", b) for b in range(16)]

    def load_xnT(h):
        hb = h % 2
        for tg in range(16):
            P.op("sync", lambda s, h=h, hb=hb, tg=tg: s.dma_start_transpose(
                xnT[hb][:, 512 * tg:512 * (tg + 1)],
                xn_hbm[512 * tg:512 * (tg + 1), 128 * h:128 * (h + 1)]),
                reads=[("xn", 4 * tg + n) for n in range(4)],
                writes=[("xnT", hb, tg)], dma=True)

    def load_xqT(h):
        P.op("sync", lambda s, h=h: s.dma_start_transpose(
            xqT[h % 2], xnq_hbm[:, 128 * h:128 * (h + 1)]),
            reads=all_xnq, writes=[("xqT", h % 2)], dma=True)

    for blk in range(NT):
        ln_block(blk, x_full[128 * blk:128 * (blk + 1), :],
                 xn_hbm[128 * blk:128 * (blk + 1), :], ("xn", blk))
    load_xnT(0)
    for blk in range(16):
        ln_block(NT + blk, x_own[128 * blk:128 * (blk + 1), :],
                 xnq_hbm[128 * blk:128 * (blk + 1), :], ("xnq", blk))
    load_xqT(0)
    P.op("dve", lambda v: v.tensor_copy(out=smaskf, in_=smask_sb),
         reads=["smask"], writes=["smaskf"])
    for w in range(64):
        P.op("dve", lambda v, w=w: v.tensor_scalar(
            out=w2[0], in0=iota_sb, scalar1=smaskf[:, w:w + 1], scalar2=None,
            op0=OP.is_ge),
            reads=["iota", "smaskf"], writes=["mstg"])
        P.op("gp", lambda g, w=w: g.dma_start(out=mask_hbm[w], in_=w2[0]),
             reads=["mstg"], writes=[("mhbm", w)], dma=True)
    prep_weights()

    nops = dict(pe=lambda t: t.nop(), act=lambda a: a.nop(), dve=lambda v: v.nop(),
                sync=lambda s: s.nop(), gp=lambda g: g.nop())
    P.barrier(nops)

    deferred = []
    for h in range(H):
        hb = h % 2
        xn_h = xnT[hb]

        def stats_mm(tc, h=h, hb=hb, xn_h=xn_h):
            k = 6 + (tc % 2)
            P.op("pe", lambda t, tc=tc, k=k, h=h: t.matmul(
                pb[k][:], xn_h[:, 128 * tc:128 * (tc + 1)], wdkvc[:, h, :],
                start=True, stop=True),
                reads=[("xnT", hb, tc // 4), ("wdkvc", h)], writes=[f"pb{k}"])
            if tc % 8 < 6:
                P.op("act", lambda a, tc=tc, k=k: a.activation(
                    out=sqdump, in_=pb[k][:], func=AF.Square, scale=cc[:, 7:8],
                    accum_out=ssq[:, tc:tc + 1]),
                    reads=[f"pb{k}", "cc"], writes=["ssq", "sqdump"])
            else:
                P.op("dve", lambda v, k=k: v.bn_stats(out=bn6, in_=pb[k][:]),
                     reads=[f"pb{k}"], writes=["bn6"])
                P.op("dve", lambda v: v.bn_aggr(out=bn2, in_=bn6),
                     reads=["bn6"], writes=["bn2"])
                P.op("dve", lambda v, tc=tc: v.tensor_copy(
                    out=ssq[:, tc:tc + 1], in_=bn2[:, 1:2]),
                    reads=["bn2"], writes=["ssq"])

        def ct_mm(tg, h=h, hb=hb, xn_h=xn_h):
            for lc in range(4):
                k = lc % 2
                P.op("pe", lambda t, lc=lc, tg=tg, k=k, h=h: t.matmul(
                    pb[k][:], wdkvc[:, h, 128 * lc:128 * (lc + 1)],
                    xn_h[:, 512 * tg:512 * (tg + 1)], start=True, stop=True),
                    reads=[("xnT", hb, tg), ("wdkvc", h)], writes=[f"pb{k}"])
                P.op("dve", lambda v, lc=lc, tg=tg, k=k: v.tensor_copy(
                    out=cT[:, lc, 512 * tg:512 * (tg + 1)], in_=pb[k][:]),
                    reads=[f"pb{k}"], writes=[("cT", lc, tg)])

        def vt_mm(t4, h=h):
            k = 2 + (t4 % 2)
            for sub in range(4):
                tc = 4 * t4 + sub
                for lc in range(4):
                    P.op("pe", lambda t, tc=tc, lc=lc, sub=sub, k=k, h=h: t.matmul(
                        pb[k][:, 128 * sub:128 * (sub + 1)],
                        cT[:, lc, 128 * tc:128 * (tc + 1)],
                        wuv2g[:, h, 128 * lc:128 * (lc + 1)],
                        start=(lc == 0), stop=(lc == 3)),
                        reads=[("cT", lc, tc // 4), ("wuv2g", h)], writes=[f"pb{k}"])
            P.op("dve", lambda v, t4=t4, k=k: v.tensor_copy(
                out=vt[:, 4 * t4:4 * (t4 + 1), :].rearrange("p a b -> p (a b)"),
                in_=pb[k][:]),
                reads=[f"pb{k}"], writes=[("vt", t4)])

        NG = S // 512
        for tg in range(NG + 1):
            if tg < NG:
                for i4 in range(4):
                    stats_mm(4 * tg + i4)
                ct_mm(tg)
            if tg >= 1:
                vt_mm(tg - 1)
        P.op("act", lambda a: a.activation(out=lnv, in_=ssq, func=AF.Ln,
                                           bias=cc[:, 0:1]),
             reads=["ssq", "cc"], writes=["lnv"])
        P.op("act", lambda a: a.activation(out=As_, in_=lnv, func=AF.Exp,
                                           scale=cc[:, 1:2], bias=cc[:, 3:4]),
             reads=["lnv", "cc"], writes=["As"])
        P.op("act", lambda a: a.mul(out=lnA, in_=lnv, mul=cc[:, 1:2]),
             reads=["lnv", "cc"], writes=["lnA"])
        P.op("act", lambda a: a.activation(out=invA, in_=lnv, func=AF.Exp,
                                           scale=cc[:, 2:3]),
             reads=["lnv", "cc"], writes=["invA"])
        if h + 1 < H:
            load_xnT(h + 1)

        while deferred:
            deferred.pop(0)[1]()

        for j in range(NJ):
            nk = 16 * (j + 1)
            for lc in range(4):
                k = 2 + (lc % 2)
                P.op("pe", lambda t, lc=lc, j=j, k=k, h=h: t.matmul(
                    pb[k][:], absorbed[:, h, 128 * lc:128 * (lc + 1)],
                    xqT[hb][:, 512 * j:512 * (j + 1)], start=True, stop=True),
                    reads=[("xqT", hb), ("abs", h)], writes=[f"pb{k}"])
                P.op("dve", lambda v, lc=lc, k=k: v.tensor_copy(
                    out=qlat[:, lc, :], in_=pb[k][:]),
                    reads=[f"pb{k}"], writes=[("qlat", lc)])
            if j == NJ - 1 and h + 1 < H:
                load_xqT(h + 1)
            pend = []
            mcnt = [0]
            SB = (0, 1, 6, 7)

            fl_n = [0]

            def flush(P=P, pend=pend, nk=nk, fl_n=fl_n):
                kk, wsel, wkey = pend.pop(0)
                fl = fl_n[0]
                fl_n[0] += 1
                ra = fl % 2
                if fl < 2:
                    P.op("dve", lambda v, kk=kk, w=wsel, ra=ra: v.tensor_scalar(
                        out=racc[ra], in0=w, scalar1=invA[:, kk:kk + 1],
                        scalar2=None, op0=OP.mult),
                        reads=[wkey, "invA"], writes=[("racc", ra)])
                else:
                    P.op("dve", lambda v, kk=kk, w=wsel, ra=ra: v.scalar_tensor_tensor(
                        out=racc[ra], in0=w, scalar=invA[:, kk:kk + 1],
                        in1=racc[ra], op0=OP.mult, op1=OP.add),
                        reads=[wkey, "invA", ("racc", ra)], writes=[("racc", ra)])
                P.op("pe", lambda t, kk=kk, w=wsel, fl=fl: t.matmul(
                    pb[5][:], vt[:, kk, :], w,
                    start=(fl == 0), stop=(fl == nk - 1)),
                    reads=[wkey, ("vt", kk // 4)], writes=["pb5"])

            full = list(range(16 * j))
            win = list(range(16 * j, nk))
            order = []
            fi = wi = 0
            while fi < len(full) or wi < len(win):
                take = max(1, len(full) // len(win))
                for _ in range(take):
                    if fi < len(full):
                        order.append(full[fi]); fi += 1
                if wi < len(win):
                    order.append(win[wi]); wi += 1
            nld = [0]

            def mload(P=P, nld=nld, j=j):
                w = 16 * j + nld[0]
                r = nld[0] % 6
                P.op("gp", lambda g, w=w, r=r: g.dma_start(
                    out=mring[r], in_=mask_hbm[w]),
                    reads=[("mhbm", w)], writes=[("mring", r)], dma=True)
                nld[0] += 1

            for _ in range(6):
                mload()
            for ci, kk in enumerate(order):
                k = SB[ci % 4]
                kw = ci % 5
                for lc in range(4):
                    P.op("pe", lambda t, kk=kk, lc=lc, k=k: t.matmul(
                        pb[k][:], cT[:, lc, 128 * kk:128 * (kk + 1)], qlat[:, lc, :],
                        start=(lc == 0), stop=(lc == 3)),
                        reads=[("cT", lc, kk // 4), ("qlat", lc)], writes=[f"pb{k}"])
                P.op("act", lambda a, kk=kk, k=k, kw=kw: a.activation(
                    out=w2[kw], in_=pb[k][:], func=AF.Exp,
                    scale=As_[:, kk:kk + 1], bias=lnA[:, kk:kk + 1]),
                    reads=[f"pb{k}", "As", "lnA"], writes=[("w2", kw)])
                while deferred and deferred[0][0] <= ci:
                    deferred.pop(0)[1]()
                while len(pend) >= 4:
                    flush()
                if kk >= 16 * j:
                    mi = mcnt[0] % 4
                    mr = (kk - 16 * j) % 6
                    mcnt[0] += 1
                    P.op("dve", lambda v, kw=kw, mi=mi, mr=mr: v.tensor_mul(
                        out=w2m[mi], in0=w2[kw], in1=mring[mr]),
                        reads=[("w2", kw), ("mring", mr)], writes=[("w2m", mi)])
                    if nld[0] < 16:
                        mload()
                    pend.append((kk, w2m[mi], ("w2m", mi)))
                else:
                    pend.append((kk, w2[kw], ("w2", kw)))
            while pend:
                flush()

            P.op("dve", lambda v: v.tensor_add(out=racc[0], in0=racc[0],
                                               in1=racc[1]),
                 reads=[("racc", 0), ("racc", 1)], writes=[("racc", 0)])

            def ep_reduce(P=P):
                P.op("pe", lambda t: t.matmul(pb[4][0:1, :], cc[:, 6:7], racc[0],
                                              start=True, stop=True),
                     reads=[("racc", 0), "cc"], writes=["pb4"])
                P.op("act", lambda a: a.activation(out=rs_row, in_=pb[4][0:1, :],
                                                   func=AF.Ln),
                     reads=["pb4"], writes=["rs_row"])

            def epilogue(P=P, h=h, j=j):
                i = j % 2
                P.op("act", lambda a: a.activation(out=rowb, in_=rs_row, func=AF.Exp,
                                                   scale=cc[0:1, 5:6]),
                     reads=["rs_row", "cc"], writes=["rowb"])
                P.op("pe", lambda t: t.matmul(pb[4][:], ones_sb, rowb,
                                              start=True, stop=True),
                     reads=["rowb", "ones"], writes=["pb4"])
                P.op("dve", lambda v: v.tensor_copy(out=rb_sb, in_=pb[4][:]),
                     reads=["pb4"], writes=["rb_sb"])
                P.op("dve", lambda v: v.tensor_mul(out=o_tmp, in0=pb[5][:], in1=rb_sb),
                     reads=["pb5", "rb_sb"], writes=["o_tmp"])
                P.op("act", lambda a: a.activation(
                    out=oh2stg[i], in_=o_tmp, func=AF.Identity, bias=vbeta[:, h:h + 1]),
                    reads=["o_tmp", ("vbeta", h)], writes=[("oh2stg", i)])
                P.op("gp", lambda g: g.dma_start(
                    out=oh2_hbm[128 * h:128 * (h + 1), 512 * j:512 * (j + 1)],
                    in_=oh2stg[i]),
                    reads=[("oh2stg", i)], writes=[("oh2", h, j)], dma=True)

            deferred.append((1, ep_reduce))
            deferred.append((3, epilogue))

    while deferred:
        deferred.pop(0)[1]()

    P.barrier(nops)

    for dc in range(16):
        P.op("sync", lambda s, dc=dc: s.dma_start_transpose(
            xnqT[:, dc, :], xnq_hbm[:, 128 * dc:128 * (dc + 1)]),
            reads=all_xnq, writes=[("xnqT", dc)], dma=True)
    P.op("gp", lambda g: g.dma_start(
        out=gwb, in_=gate_w.rearrange("(k p) e -> p k e", p=128)),
        writes=["gwb"], dma=True)
    P.op("gp", lambda g: g.dma_start(
        out=gbrow, in_=gate_b.rearrange("(a d) -> a d", a=1)),
         writes=["gbrow"], dma=True)
    for tt in range(16):
        gk = 6 + (tt % 2)
        for dc in range(16):
            P.op("pe", lambda t, tt=tt, dc=dc, gk=gk: t.matmul(
                pb[gk][:, 0:8], xnqT[:, dc, 128 * tt:128 * (tt + 1)], gwb[:, dc, :],
                start=(dc == 0), stop=False),
                reads=[("xnqT", dc), "gwb"], writes=[f"pb{gk}"])
        P.op("pe", lambda t, gk=gk: t.matmul(pb[gk][:, 0:8], ones_sb, gbrow,
                                             start=False, stop=True),
             reads=["gbrow", "ones"], writes=[f"pb{gk}"])
        P.op("dve", lambda v, gk=gk: v.tensor_copy(out=g8, in_=pb[gk][:, 0:8]),
             reads=[f"pb{gk}"], writes=["g8"])
        P.op("dve", lambda v: v.max(out=mx8, in_=g8), reads=["g8"], writes=["mx8"])
        P.op("dve", lambda v, tt=tt: v.tensor_scalar(
            out=mask_sb[:, tt, :], in0=g8, scalar1=mx8[:, 1:2], scalar2=0.5,
            op0=OP.is_ge, op1=OP.mult),
            reads=["g8", "mx8"], writes=[("mask", tt)])

    for oc in range(4):
        ocs = slice(512 * oc, 512 * (oc + 1))
        for e in range(8):
            i = e % 2
            P.op("gp", lambda g, e=e, oc=oc, i=i: g.dma_start(
                out=exsl[i],
                in_=expert_w[e, :, 512 * oc:512 * (oc + 1)].rearrange(
                    "(k p) o -> p k o", p=128)),
                writes=[("ex", i)], dma=True)
            P.op("gp", lambda g, e=e, oc=oc, i=i: g.dma_start(
                out=berow[i], in_=expert_b[e:e + 1, 512 * oc:512 * (oc + 1)]),
                writes=[("be", i)], dma=True)
            for tt in range(16):
                k = tt % 2
                for dc in range(16):
                    P.op("pe", lambda t, tt=tt, dc=dc, k=k, i=i: t.matmul(
                        pb[k][:], xnqT[:, dc, 128 * tt:128 * (tt + 1)], exsl[i][:, dc, :],
                        start=(dc == 0), stop=False),
                        reads=[("xnqT", dc), ("ex", i)], writes=[f"pb{k}"])
                P.op("pe", lambda t, k=k, i=i: t.matmul(
                    pb[k][:], ones_sb, berow[i], start=False, stop=True),
                    reads=[("be", i), "ones"], writes=[f"pb{k}"])
                if e == 0:
                    P.op("act", lambda a, tt=tt, k=k: a.activation(
                        out=acc[:, tt, :], in_=pb[k][:], func=AF.Relu,
                        scale=mask_sb[:, tt, 0:1]),
                        reads=[f"pb{k}", ("mask", tt)], writes=[("acc", tt)])
                else:
                    r = tt % 2
                    P.op("act", lambda a, tt=tt, k=k, e=e, r=r: a.activation(
                        out=relutmp[r], in_=pb[k][:], func=AF.Relu,
                        scale=mask_sb[:, tt, e:e + 1]),
                        reads=[f"pb{k}", ("mask", tt)], writes=[("rlt", r)])
                    P.op("dve", lambda v, tt=tt, r=r: v.tensor_add(
                        out=acc[:, tt, :], in0=acc[:, tt, :], in1=relutmp[r]),
                        reads=[("acc", tt), ("rlt", r)], writes=[("acc", tt)])
        for hc in range(16):
            P.op("sync", lambda s, hc=hc, oc=oc: s.dma_start_transpose(
                wot[:, hc, :], wo_bf[512 * oc:512 * (oc + 1), 128 * hc:128 * (hc + 1)]),
                reads=[("bf", "wo")], writes=[("wot", hc)], dma=True)
        P.op("gp", lambda g, ocs=ocs: g.dma_start(
            out=borow, in_=bo.rearrange("(a d) -> a d", a=1)[:, ocs]),
             writes=["borow"], dma=True)
        for j in range(NJ):
            for hc in range(16):
                P.op("sync", lambda s, hc=hc, j=j: s.dma_start(
                    out=oh2sb[:, hc, :],
                    in_=oh2_hbm[128 * hc:128 * (hc + 1), 512 * j:512 * (j + 1)]),
                    reads=[("oh2", hc, j)], writes=[("oh2sb", hc, j)], dma=True)
            for ts_ in range(4):
                tt = 4 * j + ts_
                k = 2 + (tt % 2)
                for hc in range(16):
                    P.op("pe", lambda t, hc=hc, ts_=ts_, k=k, j=j: t.matmul(
                        pb[k][:], oh2sb[:, hc, 128 * ts_:128 * (ts_ + 1)], wot[:, hc, :],
                        start=(hc == 0), stop=False),
                        reads=[("oh2sb", hc, j), ("wot", hc)], writes=[f"pb{k}"])
                P.op("pe", lambda t, k=k: t.matmul(
                    pb[k][:], ones_sb, borow, start=False, stop=True),
                    reads=["borow", "ones"], writes=[f"pb{k}"])
                i = tt % 2
                P.op("sync", lambda s, tt=tt, ocs=ocs, i=i: s.dma_start(
                    out=xres[i], in_=x_own[128 * tt:128 * (tt + 1), ocs]),
                    writes=[("xres", i)], dma=True)
                P.op("dve", lambda v, tt=tt, k=k, i=i: v.tensor_add(
                    out=relutmp[i], in0=pb[k][:], in1=acc[:, tt, :]),
                    reads=[f"pb{k}", ("acc", tt)], writes=[("rlt", i)])
                P.op("dve", lambda v, tt=tt, i=i: v.tensor_add(
                    out=outstg[i], in0=relutmp[i], in1=xres[i]),
                    reads=[("rlt", i), ("xres", i)], writes=[("outstg", i)])
                P.op("gp", lambda g, tt=tt, ocs=ocs, i=i: g.dma_start(
                    out=out_own[128 * tt:128 * (tt + 1), ocs], in_=outstg[i]),
                    reads=[("outstg", i)], writes=[("out", tt, oc)], dma=True)

    P.op("gp", None, reads=[("out", tt, oc) for tt in range(16) for oc in range(4)])

    P.finalize()

    def run_engine(handle, eng):
        for e in P.ev:
            if e["eng"] != eng:
                continue
            for peng, val in e["wl"]:
                handle.wait_ge(sem[peng], val)
            if e["fn"] is not None:
                ins = e["fn"](handle)
                if e["inc"]:
                    ins.then_inc(sem[eng], e["inc"])
            else:
                assert e["inc"] == 0

    with es:
        @block.tensor
        def _(t):
            run_engine(t, "pe")

        @block.scalar
        def _(a):
            run_engine(a, "act")

        @block.vector
        def _(v):
            run_engine(v, "dve")

        @block.sync
        def _(s):
            run_engine(s, "sync")

        @block.gpsimd
        def _(g):
            run_engine(g, "gp")

    return nc


_NC_CACHE = None


def kernel(L0_attn_x, L0_attn_Wq, L0_attn_Wdkv, L0_attn_Wuk, L0_attn_Wuv,
           L0_attn_ln_g, L0_attn_ln_b, L0_attn_Wo_h, L0_attn_Wo, L0_attn_bo,
           L0_ffn_gate_w, L0_ffn_gate_b, L0_ffn_expert_w, L0_ffn_expert_b):
    global _NC_CACHE
    x = np.ascontiguousarray(L0_attn_x, np.float32)
    iota_q = np.broadcast_to(np.arange(512, dtype=np.float16), (128, 512)).copy()
    ones_r = np.ones((1, 128), dtype=BF)
    lngcol = np.ascontiguousarray(
        np.asarray(L0_attn_ln_g, np.float32).reshape(16, 4, 128).transpose(2, 0, 1)
        .reshape(128, 64))
    lnbcol = np.ascontiguousarray(
        np.asarray(L0_attn_ln_b, np.float32).reshape(16, 4, 128).transpose(2, 0, 1)
        .reshape(128, 64))
    shared = {
        "Wq": np.ascontiguousarray(L0_attn_Wq, np.float32),
        "Wdkv": np.ascontiguousarray(L0_attn_Wdkv, np.float32),
        "Wuk": np.ascontiguousarray(L0_attn_Wuk, np.float32),
        "Wuv": np.ascontiguousarray(L0_attn_Wuv, np.float32),
        "Wo_h": np.ascontiguousarray(L0_attn_Wo_h, np.float32),
        "Wo": np.ascontiguousarray(L0_attn_Wo, np.float32),
        "bo": np.ascontiguousarray(L0_attn_bo, np.float32),
        "gate_w": np.ascontiguousarray(L0_ffn_gate_w, np.float32),
        "gate_b": np.ascontiguousarray(L0_ffn_gate_b, np.float32),
        "expert_w": np.ascontiguousarray(L0_ffn_expert_w, np.float32),
        "expert_b": np.ascontiguousarray(L0_ffn_expert_b, np.float32),
        "lngcol": lngcol, "lnbcol": lnbcol,
        "iota_q": iota_q, "ones_r": ones_r,
        "consts": np.tile(np.array([[EPS, -0.5, 0.5, LN_ISQ, 1.0 / L,
                                     -1.0, 1.0, ISQ]], np.float32), (128, 1)),
    }
    in_maps = []
    groups_all = []
    for core in range(8):
        b, ii = core // 4, core % 4
        groups = [ii, 7 - ii, 8 + ii, 15 - ii]
        groups_all.append((b, groups))
        x_own = np.ascontiguousarray(
            np.concatenate([x[b, 512 * g:512 * (g + 1)] for g in groups], axis=0))
        sm = np.empty((128, 64), np.float16)
        p = np.arange(128)
        for j in range(4):
            q0 = 512 * groups[j]
            for kk in range(16):
                t0 = 128 * (16 * j + kk)
                sm[:, 16 * j + kk] = (t0 + p - q0).astype(np.float16)
        im = dict(shared)
        im["x_full"] = x[b]
        im["x_own"] = x_own
        im["smask"] = sm
        in_maps.append(im)

    if _NC_CACHE is None:
        _NC_CACHE = build_nc()
    res = run_bass_kernel_spmd(_NC_CACHE, in_maps, core_ids=list(range(8)))

    out = np.empty((2, S, D), np.float32)
    for core in range(8):
        b, groups = groups_all[core]
        r = res.results[core]["out_own"]
        for j, g in enumerate(groups):
            out[b, 512 * g:512 * (g + 1)] = r[512 * j:512 * (j + 1)]
    return out


# version 18
# speedup vs baseline: 1.2497x; 1.0331x over previous
import contextlib
import numpy as np
import ml_dtypes
import concourse.bass as bass
import concourse.mybir as mybir
from concourse.bass_utils import run_bass_kernel_spmd

F32 = mybir.dt.float32
BF16 = mybir.dt.bfloat16
FP16 = mybir.dt.float16
U8 = mybir.dt.uint8
AF = mybir.ActivationFunctionType
OP = mybir.AluOpType
AX = mybir.AxisListType.X
BF = ml_dtypes.bfloat16

S, D, H, L, HD = 8192, 2048, 16, 512, 128
NJ = 4
OWN = 2048
NT = S // 128
EPS = 1e-5
ISQ = float(1.0 / np.sqrt(np.float64(L)))
LN_ISQ = float(np.log(1.0 / np.sqrt(np.float64(L))))

ENGS = ("pe", "act", "dve", "sync", "gp")


class Plan:

    def __init__(self):
        self.ev = []
        self.last_w = {}
        self.last_r = {}

    def op(self, eng, fn, reads=(), writes=(), dma=False):
        i = len(self.ev)
        deps = set()
        for k in reads:
            if k in self.last_w:
                deps.add(self.last_w[k])
        for k in writes:
            if k in self.last_w:
                deps.add(self.last_w[k])
            for j in self.last_r.get(k, {}).values():
                deps.add(j)
        self.ev.append(dict(i=i, eng=eng, fn=fn, deps=deps, dma=dma))
        for k in reads:
            self.last_r.setdefault(k, {})[eng] = i
        for k in writes:
            self.last_w[k] = i
            self.last_r[k] = {}
        return i

    def barrier(self, nops):
        bid = len(self.ev)
        pending = list(self.last_w.keys())
        for e in ENGS:
            self.op(e, nops[e], reads=pending, writes=[("bar", bid, e)])
        for e in ENGS:
            self.op(e, None, reads=[("bar", bid, e2) for e2 in ENGS if e2 != e])

    def finalize(self):
        need = [False] * len(self.ev)
        for e in self.ev:
            byeng = {}
            for d in sorted(e["deps"]):
                p = self.ev[d]
                if p["eng"] == e["eng"] == "pe":
                    continue
                if byeng.get(p["eng"], -1) < d:
                    byeng[p["eng"]] = d
            e["waits"] = byeng
            for d in byeng.values():
                need[d] = True
        cnt = {g: 0 for g in ENGS}
        for e in self.ev:
            if e["dma"]:
                need[e["i"]] = True
            if need[e["i"]]:
                e["inc"] = 16 if e["dma"] else 1
                cnt[e["eng"]] += e["inc"]
            else:
                e["inc"] = 0
            e["cum"] = cnt[e["eng"]]
        water = {}
        for e in self.ev:
            wl = []
            for peng in sorted(e["waits"]):
                val = self.ev[e["waits"][peng]]["cum"]
                assert self.ev[e["waits"][peng]]["inc"] > 0
                key = (e["eng"], peng)
                if water.get(key, -1) >= val:
                    continue
                water[key] = val
                wl.append((peng, val))
            e["wl"] = wl
        self.totals = cnt


def build_nc():
    nc = bass.Bass("TRN2", target_bir_lowering=False, debug=False)

    def din(name, shape, dt=F32):
        return nc.dram_tensor(name, shape, dt, kind="ExternalInput").ap()

    x_full = din("x_full", [S, D])
    x_own = din("x_own", [OWN, D])
    Wq = din("Wq", [H, HD, HD])
    Wdkv = din("Wdkv", [H, L, HD])
    Wuk = din("Wuk", [H, L, HD])
    Wuv = din("Wuv", [H, L, HD])
    Wo_h = din("Wo_h", [H, HD, HD])
    Wo = din("Wo", [D, D])
    bo = din("bo", [D])
    gate_w = din("gate_w", [D, 8])
    gate_b = din("gate_b", [8])
    expert_w = din("expert_w", [8, D, D])
    expert_b = din("expert_b", [8, D])
    lngcol = din("lngcol", [128, 64])
    lnbcol = din("lnbcol", [128, 64])
    iota_q = din("iota_q", [128, 512], FP16)
    smask = din("smask", [128, 64], FP16)
    ones_r = din("ones_r", [1, 128], BF16)
    consts = din("consts", [128, 8])

    out_own = nc.dram_tensor("out_own", [OWN, D], F32, kind="ExternalOutput").ap()

    def dint(name, shape, dt=BF16):
        return nc.dram_tensor(name, shape, dt, kind="Internal").ap()

    xn_hbm = dint("xn_hbm", [S, D])
    xnq_hbm = dint("xnq_hbm", [OWN, D])
    oh2_hbm = dint("oh2_hbm", [D, OWN])
    mask_hbm = dint("mask_hbm", [64, 128, 512])
    wq_bf = dint("wq_bf", [H, HD, HD])
    wdkv_bf = dint("wdkv_bf", [H, L, HD])
    wukp_bf = dint("wukp_bf", [H, L, HD])
    wuv_bf = dint("wuv_bf", [H, L, HD])
    woh_bf = dint("woh_bf", [H, HD, HD])
    wo_bf = dint("wo_bf", [D, D])

    ARENA_BYTES = 210752
    backing = nc.alloc_sbuf_tensor("arena", [128, ARENA_BYTES], U8)
    base = nc.lookup_mloc(backing).addr
    cur = [0]

    def place(name, shape, dt, off=None):
        nbytes = int(np.prod(shape[1:])) * mybir.dt.size(dt)
        if off is None:
            off = cur[0]
            cur[0] = (off + nbytes + 31) & ~31
        else:
            assert off + nbytes <= ARENA_BYTES, name
        assert off + nbytes <= ARENA_BYTES, (name, off, nbytes)
        h = nc.virtual_sbuf_tensor(name, list(shape), dt)
        nc._tensor(h.name, h.shape, h.dtype, type="SB")
        mloc = nc.lookup_mloc(h)
        mloc.allocated = True
        mloc.addr = base + off
        return h.ap()

    A1, A2, A3, A4, A5, A6 = 0, 65536, 98304, 114688, 131072, 147456
    ST = 163840
    cur[0] = ST
    x_t = [place(f"x_t{n}", [128, D], F32, A1 + 8192 * n) for n in range(4)]
    xnout = [place(f"xnout{n}", [128, D], BF16, A1 + 32768 + 4096 * n)
             for n in range(4)]
    cT = place("cT", [128, 4, S], BF16, A1)
    xnT = [place("xnT0", [128, S], BF16, A2), place("xnT1", [128, S], BF16, A2 + 16384)]
    vt = place("vt", [128, NT, 128], BF16, A3)
    wdkvc = place("wdkvc", [128, H, L], BF16, A4)
    absorbed = place("absorbed", [128, H, L], BF16, A5)
    wuv2g = place("wuv2g", [128, H, L], BF16, A6)
    o = A3
    wukstg = [place("wukstg0", [128, 4, 128], F32, o),
              place("wukstg1", [128, 4, 128], F32, o + 2048)]
    wukpstg = [place("wukpstg0", [128, 4, 128], BF16, o + 4096),
               place("wukpstg1", [128, 4, 128], BF16, o + 5120)]
    wqT = place("wqT", [128, 128], BF16, o + 6144)
    wukpT = place("wukpT", [128, 512], BF16, o + 6400)
    wdkvT = place("wdkvT", [128, 512], BF16, o + 7424)
    wuvT = place("wuvT", [128, 512], BF16, o + 8448)
    wohT = place("wohT", [128, 128], BF16, o + 9472)
    wuv2raw = place("wuv2raw", [128, 512], BF16, o + 9728)
    msum = place("msum", [128, 1], F32, o + 10752)
    mcol = place("mcol", [128, 1], F32, o + 10784)
    xnqT = place("xnqT", [128, 16, D], BF16, A1)
    acc = place("acc", [128, 16, 512], F32, A2)
    oh2sb = place("oh2sb", [128, 16, 512], BF16, A3)
    wot = place("wot", [128, 16, 512], BF16, A4)
    exsl = [place("ex0", [128, 16, 512], BF16, A5), place("ex1", [128, 16, 512], BF16, A6)]

    xqT = [place("xqT0", [128, OWN], BF16), place("xqT1", [128, OWN], BF16)]
    ones_sb = place("ones_sb", [1, 128], BF16)
    cc = place("cc_sb", [128, 8], F32)
    lng_sb = place("lng_sb", [128, 64], F32)
    lnb_sb = place("lnb_sb", [128, 64], F32)
    lnbb_sb = place("lnbb_sb", [128, 64], BF16)
    vbeta = place("vbeta", [128, H], F32)
    iota_sb = place("iota_sb", [128, 512], FP16)
    smask_sb = place("smask_sb", [128, 64], FP16)
    smaskf = place("smaskf", [128, 64], F32)
    ph2s = cur[0]
    w2 = [place(f"w2_{n}", [128, 512], BF16) for n in range(5)]
    w2m = [place(f"w2m{n}", [128, 512], BF16) for n in range(3)]
    qlat = place("qlat", [128, 4, 512], BF16)
    ssq = place("ssq", [128, NT], F32)
    lnv = place("lnv", [128, NT], F32)
    As_ = place("As_", [128, NT], F32)
    lnA = place("lnA", [128, NT], F32)
    invA = place("invA", [128, NT], F32)
    sqdump = place("sqdump", [128, 512], F32)
    rs_row = place("rs_row", [1, 512], F32)
    rowb = place("rowb", [1, 512], BF16)
    rb_sb = place("rb_sb", [128, 512], F32)
    racc = [place("racc0", [128, 512], F32), place("racc1", [128, 512], F32)]
    mring = [place(f"mring{n}", [128, 512], BF16) for n in range(6)]
    o_tmp = place("o_tmp", [128, 512], F32)
    oh2stg = [place("oh2stg0", [128, 512], BF16), place("oh2stg1", [128, 512], BF16)]
    st24 = [place(f"st24_{n}", [128, 24], F32) for n in range(4)]
    st2 = [place(f"st2_{n}", [128, 2], F32) for n in range(4)]
    nm1 = [place(f"nm1_{n}", [128, 1], F32) for n in range(4)]
    lnv1 = [place(f"lnv1_{n}", [128, 1], F32) for n in range(4)]
    rs1 = [place(f"rs1_{n}", [128, 1], F32) for n in range(4)]
    nmrs = [place(f"nmrs_{n}", [128, 1], F32) for n in range(4)]
    ph2e = cur[0]
    assert ph2e <= ARENA_BYTES, ph2e
    cur[0] = ph2s
    g8 = place("g8", [128, 8], F32)
    mx8 = place("mx8", [128, 8], F32)
    mask_sb = place("mask_sb", [128, 16, 8], F32)
    gwb = place("gwb", [128, 16, 8], BF16)
    gbrow = place("gbrow", [1, 8], BF16)
    borow = place("borow", [1, 512], BF16)
    berow = [place("berow0", [1, 512], BF16), place("berow1", [1, 512], BF16)]
    xres = [place("xres0", [128, 512], F32), place("xres1", [128, 512], F32)]
    outstg = [place("outstg0", [128, 512], F32), place("outstg1", [128, 512], F32)]
    relutmp = [place("rlt0", [128, 512], F32), place("rlt1", [128, 512], F32)]
    assert cur[0] <= ph2e, (cur[0], ph2e)

    pb = [nc.alloc_psum_tensor(f"pb{k}", [128, 512], F32).ap() for k in range(8)]

    es = contextlib.ExitStack()
    sem = {g: es.enter_context(nc.semaphore(f"sem_{g}")) for g in ENGS}
    block = es.enter_context(nc.Block())

    P = Plan()
    KW = dict()

    for nm, src, dst in (("wq", Wq, wq_bf), ("wdkv", Wdkv, wdkv_bf),
                         ("wuv", Wuv, wuv_bf), ("woh", Wo_h, woh_bf),
                         ("wo", Wo, wo_bf)):
        P.op("gp", lambda g, s=src, d=dst: g.dma_start(out=d, in_=s),
             writes=[("bf", nm)], dma=True)
    P.op("sync", lambda s: s.dma_start(out=iota_sb, in_=iota_q), writes=["iota"], dma=True)
    P.op("sync", lambda s: s.dma_start(out=smask_sb, in_=smask), writes=["smask"], dma=True)
    P.op("sync", lambda s: s.dma_start(out=lng_sb, in_=lngcol), writes=["lng"], dma=True)
    P.op("sync", lambda s: s.dma_start(out=lnb_sb, in_=lnbcol), writes=["lnb"], dma=True)
    P.op("sync", lambda s: s.dma_start(out=ones_sb, in_=ones_r), writes=["ones"], dma=True)
    P.op("sync", lambda s: s.dma_start(out=cc, in_=consts), writes=["cc"], dma=True)
    P.op("dve", lambda v: v.tensor_copy(out=lnbb_sb, in_=lnb_sb),
         reads=["lnb"], writes=["lnbb"])
    def prep_wdkvc():
        for h in range(H):
            P.op("sync", lambda s, h=h: s.dma_start_transpose(wdkvT, wdkv_bf[h]),
                 reads=[("bf", "wdkv")], writes=["wdkvT"], dma=True)
            P.op("dve", lambda v: v.reduce_sum(out=msum, in_=wdkvT, axis=AX),
                 reads=["wdkvT"], writes=["msum"])
            P.op("dve", lambda v: v.tensor_scalar(out=mcol, in0=msum, scalar1=-1.0 / L,
                                                  scalar2=None, op0=OP.mult),
                 reads=["msum"], writes=["mcol"])
            P.op("dve", lambda v, h=h: v.tensor_scalar(
                out=wdkvc[:, h, :], in0=wdkvT, scalar1=mcol, scalar2=None, op0=OP.add),
                reads=["wdkvT", "mcol"], writes=[("wdkvc", h)])

    def prep_weights():
      for h in range(H):
        i = h % 2
        P.op("gp", lambda g, h=h, i=i: g.dma_start(
            out=wukstg[i], in_=Wuk[h].rearrange("(lc p) j -> p lc j", p=128)),
            writes=[("wukstg", i)], dma=True)
        for lc in range(4):
            P.op("dve", lambda v, h=h, lc=lc, i=i: v.tensor_scalar(
                out=wukpstg[i][:, lc, :], in0=wukstg[i][:, lc, :],
                scalar1=lng_sb[:, h * 4 + lc:h * 4 + lc + 1],
                scalar2=None, op0=OP.mult),
                reads=[("wukstg", i), "lng"], writes=[("wukpstg", i)])
        P.op("gp", lambda g, h=h, i=i: g.dma_start(
            out=wukp_bf[h].rearrange("(lc p) j -> p lc j", p=128), in_=wukpstg[i]),
            reads=[("wukpstg", i)], writes=[("wukp", h)], dma=True)
      for h in range(H):
        P.op("sync", lambda s, h=h: s.dma_start_transpose(wqT, wq_bf[h]),
             reads=[("bf", "wq")], writes=["wqT"], dma=True)
        P.op("sync", lambda s, h=h: s.dma_start_transpose(wukpT, wukp_bf[h]),
             reads=[("wukp", h)], writes=["wukpT"], dma=True)
        P.op("pe", lambda t: t.matmul(pb[4][:], wqT, wukpT, start=True, stop=True),
             reads=["wqT", "wukpT"], writes=["pb4"])
        P.op("dve", lambda v, h=h: v.tensor_copy(out=absorbed[:, h, :], in_=pb[4][:]),
             reads=["pb4"], writes=[("abs", h)])
        P.op("sync", lambda s, h=h: s.dma_start_transpose(wuvT, wuv_bf[h]),
             reads=[("bf", "wuv")], writes=["wuvT"], dma=True)
        P.op("sync", lambda s, h=h: s.dma_start_transpose(wohT, woh_bf[h]),
             reads=[("bf", "woh")], writes=["wohT"], dma=True)
        for lc in range(4):
            P.op("pe", lambda t, lc=lc: t.matmul(
                pb[5][:, 128 * lc:128 * (lc + 1)], wuvT[:, 128 * lc:128 * (lc + 1)],
                wohT, start=True, stop=True),
                reads=["wuvT", "wohT"], writes=["pb5"])
        P.op("dve", lambda v: v.tensor_copy(out=wuv2raw, in_=pb[5][:]),
             reads=["pb5"], writes=["wuv2raw"])
        for lc in range(4):
            P.op("dve", lambda v, h=h, lc=lc: v.tensor_scalar(
                out=wuv2g[:, h, 128 * lc:128 * (lc + 1)],
                in0=wuv2raw[:, 128 * lc:128 * (lc + 1)],
                scalar1=lng_sb[:, h * 4 + lc:h * 4 + lc + 1], scalar2=None, op0=OP.mult),
                reads=["wuv2raw", "lng"], writes=[("wuv2g", h)])
        for lc in range(4):
            P.op("pe", lambda t, h=h, lc=lc: t.matmul(
                pb[4][:, 0:1], wuv2raw[:, 128 * lc:128 * (lc + 1)],
                lnbb_sb[:, h * 4 + lc:h * 4 + lc + 1],
                start=(lc == 0), stop=(lc == 3)),
                reads=["wuv2raw", "lnbb"], writes=["pb4"])
        P.op("dve", lambda v, h=h: v.tensor_copy(out=vbeta[:, h:h + 1], in_=pb[4][:, 0:1]),
             reads=["pb4"], writes=[("vbeta", h)])

    prep_wdkvc()

    def ln_block(blk, src_rows, dst, key):
        i = blk % 4
        P.op("sync", lambda s, r=src_rows, i=i: s.dma_start(out=x_t[i], in_=r),
             writes=[("x", i)], dma=True)
        for q in range(4):
            P.op("dve", lambda v, i=i, q=q: v.bn_stats(
                out=st24[i][:, 6 * q:6 * (q + 1)], in_=x_t[i][:, 512 * q:512 * (q + 1)]),
                reads=[("x", i)], writes=[("st24", i)])
        P.op("dve", lambda v, i=i: v.bn_aggr(out=st2[i], in_=st24[i]),
             reads=[("st24", i)], writes=[("st2", i)])
        P.op("dve", lambda v, i=i: v.tensor_scalar(out=nm1[i], in0=st2[i][:, 0:1],
                                                   scalar1=-1.0, scalar2=None, op0=OP.mult),
             reads=[("st2", i)], writes=[("nm1", i)])
        P.op("act", lambda a, i=i: a.activation(out=lnv1[i], in_=st2[i][:, 1:2], func=AF.Ln,
                                                bias=cc[:, 0:1]),
             reads=[("st2", i), "cc"], writes=[("lnv1", i)])
        P.op("act", lambda a, i=i: a.activation(out=rs1[i], in_=lnv1[i], func=AF.Exp,
                                                scale=cc[:, 1:2]),
             reads=[("lnv1", i), "cc"], writes=[("rs1", i)])
        P.op("act", lambda a, i=i: a.mul(out=nmrs[i], in_=rs1[i], mul=nm1[i]),
             reads=[("rs1", i), ("nm1", i)], writes=[("nmrs", i)])
        P.op("act", lambda a, i=i: a.activation(out=xnout[i], in_=x_t[i], func=AF.Identity,
                                                scale=rs1[i], bias=nmrs[i]),
             reads=[("x", i), ("rs1", i), ("nmrs", i)], writes=[("xnout", i)])
        P.op("gp", lambda g, d=dst, i=i: g.dma_start(out=d, in_=xnout[i]),
             reads=[("xnout", i)], writes=[key], dma=True)

    all_xnq = [("xnq", b) for b in range(16)]

    def load_xnT(h):
        hb = h % 2
        for tg in range(16):
            P.op("sync", lambda s, h=h, hb=hb, tg=tg: s.dma_start_transpose(
                xnT[hb][:, 512 * tg:512 * (tg + 1)],
                xn_hbm[512 * tg:512 * (tg + 1), 128 * h:128 * (h + 1)]),
                reads=[("xn", 4 * tg + n) for n in range(4)],
                writes=[("xnT", hb, tg)], dma=True)

    def load_xqT(h):
        P.op("sync", lambda s, h=h: s.dma_start_transpose(
            xqT[h % 2], xnq_hbm[:, 128 * h:128 * (h + 1)]),
            reads=all_xnq, writes=[("xqT", h % 2)], dma=True)

    for blk in range(NT):
        ln_block(blk, x_full[128 * blk:128 * (blk + 1), :],
                 xn_hbm[128 * blk:128 * (blk + 1), :], ("xn", blk))
    load_xnT(0)
    for blk in range(16):
        ln_block(NT + blk, x_own[128 * blk:128 * (blk + 1), :],
                 xnq_hbm[128 * blk:128 * (blk + 1), :], ("xnq", blk))
    load_xqT(0)
    P.op("dve", lambda v: v.tensor_copy(out=smaskf, in_=smask_sb),
         reads=["smask"], writes=["smaskf"])
    for w in range(64):
        P.op("dve", lambda v, w=w: v.tensor_scalar(
            out=w2[0], in0=iota_sb, scalar1=smaskf[:, w:w + 1], scalar2=None,
            op0=OP.is_ge),
            reads=["iota", "smaskf"], writes=["mstg"])
        P.op("gp", lambda g, w=w: g.dma_start(out=mask_hbm[w], in_=w2[0]),
             reads=["mstg"], writes=[("mhbm", w)], dma=True)
    prep_weights()

    nops = dict(pe=lambda t: t.nop(), act=lambda a: a.nop(), dve=lambda v: v.nop(),
                sync=lambda s: s.nop(), gp=lambda g: g.nop())
    P.barrier(nops)

    deferred = [None]
    for h in range(H):
        hb = h % 2
        xn_h = xnT[hb]

        def stats_mm(tc, h=h, hb=hb, xn_h=xn_h):
            k = 6 + (tc % 2)
            P.op("pe", lambda t, tc=tc, k=k, h=h: t.matmul(
                pb[k][:], xn_h[:, 128 * tc:128 * (tc + 1)], wdkvc[:, h, :],
                start=True, stop=True),
                reads=[("xnT", hb, tc // 4), ("wdkvc", h)], writes=[f"pb{k}"])
            P.op("act", lambda a, tc=tc, k=k: a.activation(
                out=sqdump, in_=pb[k][:], func=AF.Square, accum_out=ssq[:, tc:tc + 1]),
                reads=[f"pb{k}"], writes=["ssq", "sqdump"])

        def ct_mm(tg, h=h, hb=hb, xn_h=xn_h):
            for lc in range(4):
                k = lc % 2
                P.op("pe", lambda t, lc=lc, tg=tg, k=k, h=h: t.matmul(
                    pb[k][:], wdkvc[:, h, 128 * lc:128 * (lc + 1)],
                    xn_h[:, 512 * tg:512 * (tg + 1)], start=True, stop=True),
                    reads=[("xnT", hb, tg), ("wdkvc", h)], writes=[f"pb{k}"])
                P.op("dve", lambda v, lc=lc, tg=tg, k=k: v.tensor_copy(
                    out=cT[:, lc, 512 * tg:512 * (tg + 1)], in_=pb[k][:]),
                    reads=[f"pb{k}"], writes=[("cT", lc, tg)])

        def vt_mm(t4, h=h):
            k = 2 + (t4 % 2)
            for sub in range(4):
                tc = 4 * t4 + sub
                for lc in range(4):
                    P.op("pe", lambda t, tc=tc, lc=lc, sub=sub, k=k, h=h: t.matmul(
                        pb[k][:, 128 * sub:128 * (sub + 1)],
                        cT[:, lc, 128 * tc:128 * (tc + 1)],
                        wuv2g[:, h, 128 * lc:128 * (lc + 1)],
                        start=(lc == 0), stop=(lc == 3)),
                        reads=[("cT", lc, tc // 4), ("wuv2g", h)], writes=[f"pb{k}"])
            P.op("dve", lambda v, t4=t4, k=k: v.tensor_copy(
                out=vt[:, 4 * t4:4 * (t4 + 1), :].rearrange("p a b -> p (a b)"),
                in_=pb[k][:]),
                reads=[f"pb{k}"], writes=[("vt", t4)])

        NG = S // 512
        for tg in range(NG + 1):
            if tg < NG:
                for i4 in range(4):
                    stats_mm(4 * tg + i4)
                ct_mm(tg)
            if tg >= 1:
                vt_mm(tg - 1)
        P.op("act", lambda a: a.activation(out=lnv, in_=ssq, func=AF.Ln,
                                           scale=cc[:, 4:5], bias=cc[:, 0:1]),
             reads=["ssq", "cc"], writes=["lnv"])
        P.op("act", lambda a: a.activation(out=As_, in_=lnv, func=AF.Exp,
                                           scale=cc[:, 1:2], bias=cc[:, 3:4]),
             reads=["lnv", "cc"], writes=["As"])
        P.op("act", lambda a: a.mul(out=lnA, in_=lnv, mul=cc[:, 1:2]),
             reads=["lnv", "cc"], writes=["lnA"])
        P.op("act", lambda a: a.activation(out=invA, in_=lnv, func=AF.Exp,
                                           scale=cc[:, 2:3]),
             reads=["lnv", "cc"], writes=["invA"])
        if h + 1 < H:
            load_xnT(h + 1)

        if deferred[0] is not None:
            deferred[0]()
            deferred[0] = None

        for j in range(NJ):
            nk = 16 * (j + 1)
            for lc in range(4):
                k = 2 + (lc % 2)
                P.op("pe", lambda t, lc=lc, j=j, k=k, h=h: t.matmul(
                    pb[k][:], absorbed[:, h, 128 * lc:128 * (lc + 1)],
                    xqT[hb][:, 512 * j:512 * (j + 1)], start=True, stop=True),
                    reads=[("xqT", hb), ("abs", h)], writes=[f"pb{k}"])
                P.op("dve", lambda v, lc=lc, k=k: v.tensor_copy(
                    out=qlat[:, lc, :], in_=pb[k][:]),
                    reads=[f"pb{k}"], writes=[("qlat", lc)])
            if j == NJ - 1 and h + 1 < H:
                load_xqT(h + 1)
            pend = []
            mcnt = [0]
            SB = (0, 1, 6, 7)

            fl_n = [0]

            def flush(P=P, pend=pend, nk=nk, fl_n=fl_n):
                kk, wsel, wkey = pend.pop(0)
                fl = fl_n[0]
                fl_n[0] += 1
                ra = fl % 2
                if fl < 2:
                    P.op("dve", lambda v, kk=kk, w=wsel, ra=ra: v.tensor_scalar(
                        out=racc[ra], in0=w, scalar1=invA[:, kk:kk + 1],
                        scalar2=None, op0=OP.mult),
                        reads=[wkey, "invA"], writes=[("racc", ra)])
                else:
                    P.op("dve", lambda v, kk=kk, w=wsel, ra=ra: v.scalar_tensor_tensor(
                        out=racc[ra], in0=w, scalar=invA[:, kk:kk + 1],
                        in1=racc[ra], op0=OP.mult, op1=OP.add),
                        reads=[wkey, "invA", ("racc", ra)], writes=[("racc", ra)])
                P.op("pe", lambda t, kk=kk, w=wsel, fl=fl: t.matmul(
                    pb[5][:], vt[:, kk, :], w,
                    start=(fl == 0), stop=(fl == nk - 1)),
                    reads=[wkey, ("vt", kk // 4)], writes=["pb5"])

            full = list(range(16 * j))
            win = list(range(16 * j, nk))
            order = []
            fi = wi = 0
            while fi < len(full) or wi < len(win):
                take = max(1, len(full) // len(win))
                for _ in range(take):
                    if fi < len(full):
                        order.append(full[fi]); fi += 1
                if wi < len(win):
                    order.append(win[wi]); wi += 1
            nld = [0]

            def mload(P=P, nld=nld, j=j):
                w = 16 * j + nld[0]
                r = nld[0] % 6
                P.op("gp", lambda g, w=w, r=r: g.dma_start(
                    out=mring[r], in_=mask_hbm[w]),
                    reads=[("mhbm", w)], writes=[("mring", r)], dma=True)
                nld[0] += 1

            for _ in range(6):
                mload()
            for ci, kk in enumerate(order):
                k = SB[ci % 4]
                kw = ci % 5
                for lc in range(4):
                    P.op("pe", lambda t, kk=kk, lc=lc, k=k: t.matmul(
                        pb[k][:], cT[:, lc, 128 * kk:128 * (kk + 1)], qlat[:, lc, :],
                        start=(lc == 0), stop=(lc == 3)),
                        reads=[("cT", lc, kk // 4), ("qlat", lc)], writes=[f"pb{k}"])
                P.op("act", lambda a, kk=kk, k=k, kw=kw: a.activation(
                    out=w2[kw], in_=pb[k][:], func=AF.Exp,
                    scale=As_[:, kk:kk + 1], bias=lnA[:, kk:kk + 1]),
                    reads=[f"pb{k}", "As", "lnA"], writes=[("w2", kw)])
                if ci == 2 and deferred[0] is not None:
                    deferred[0]()
                    deferred[0] = None
                while len(pend) >= 3:
                    flush()
                if kk >= 16 * j:
                    mi = mcnt[0] % 3
                    mr = (kk - 16 * j) % 6
                    mcnt[0] += 1
                    P.op("dve", lambda v, kw=kw, mi=mi, mr=mr: v.tensor_mul(
                        out=w2m[mi], in0=w2[kw], in1=mring[mr]),
                        reads=[("w2", kw), ("mring", mr)], writes=[("w2m", mi)])
                    if nld[0] < 16:
                        mload()
                    pend.append((kk, w2m[mi], ("w2m", mi)))
                else:
                    pend.append((kk, w2[kw], ("w2", kw)))
            while pend:
                flush()

            def epilogue(P=P, h=h, j=j):
                i = j % 2
                P.op("dve", lambda v: v.tensor_add(out=racc[0], in0=racc[0],
                                                   in1=racc[1]),
                     reads=[("racc", 0), ("racc", 1)], writes=[("racc", 0)])
                P.op("pe", lambda t: t.matmul(pb[4][0:1, :], cc[:, 6:7], racc[0],
                                              start=True, stop=True),
                     reads=[("racc", 0), "cc"], writes=["pb4"])
                P.op("act", lambda a: a.activation(out=rs_row, in_=pb[4][0:1, :],
                                                   func=AF.Ln),
                     reads=["pb4"], writes=["rs_row"])
                P.op("act", lambda a: a.activation(out=rowb, in_=rs_row, func=AF.Exp,
                                                   scale=cc[0:1, 5:6]),
                     reads=["rs_row", "cc"], writes=["rowb"])
                P.op("pe", lambda t: t.matmul(pb[4][:], ones_sb, rowb,
                                              start=True, stop=True),
                     reads=["rowb", "ones"], writes=["pb4"])
                P.op("dve", lambda v: v.tensor_copy(out=rb_sb, in_=pb[4][:]),
                     reads=["pb4"], writes=["rb_sb"])
                P.op("dve", lambda v: v.tensor_mul(out=o_tmp, in0=pb[5][:], in1=rb_sb),
                     reads=["pb5", "rb_sb"], writes=["o_tmp"])
                P.op("act", lambda a: a.activation(
                    out=oh2stg[i], in_=o_tmp, func=AF.Identity, bias=vbeta[:, h:h + 1]),
                    reads=["o_tmp", ("vbeta", h)], writes=[("oh2stg", i)])
                P.op("gp", lambda g: g.dma_start(
                    out=oh2_hbm[128 * h:128 * (h + 1), 512 * j:512 * (j + 1)],
                    in_=oh2stg[i]),
                    reads=[("oh2stg", i)], writes=[("oh2", h, j)], dma=True)

            deferred[0] = epilogue

    if deferred[0] is not None:
        deferred[0]()
        deferred[0] = None

    P.barrier(nops)

    for dc in range(16):
        P.op("sync", lambda s, dc=dc: s.dma_start_transpose(
            xnqT[:, dc, :], xnq_hbm[:, 128 * dc:128 * (dc + 1)]),
            reads=all_xnq, writes=[("xnqT", dc)], dma=True)
    P.op("gp", lambda g: g.dma_start(
        out=gwb, in_=gate_w.rearrange("(k p) e -> p k e", p=128)),
        writes=["gwb"], dma=True)
    P.op("gp", lambda g: g.dma_start(
        out=gbrow, in_=gate_b.rearrange("(a d) -> a d", a=1)),
         writes=["gbrow"], dma=True)
    for tt in range(16):
        gk = 6 + (tt % 2)
        for dc in range(16):
            P.op("pe", lambda t, tt=tt, dc=dc, gk=gk: t.matmul(
                pb[gk][:, 0:8], xnqT[:, dc, 128 * tt:128 * (tt + 1)], gwb[:, dc, :],
                start=(dc == 0), stop=False),
                reads=[("xnqT", dc), "gwb"], writes=[f"pb{gk}"])
        P.op("pe", lambda t, gk=gk: t.matmul(pb[gk][:, 0:8], ones_sb, gbrow,
                                             start=False, stop=True),
             reads=["gbrow", "ones"], writes=[f"pb{gk}"])
        P.op("dve", lambda v, gk=gk: v.tensor_copy(out=g8, in_=pb[gk][:, 0:8]),
             reads=[f"pb{gk}"], writes=["g8"])
        P.op("dve", lambda v: v.max(out=mx8, in_=g8), reads=["g8"], writes=["mx8"])
        P.op("dve", lambda v, tt=tt: v.tensor_scalar(
            out=mask_sb[:, tt, :], in0=g8, scalar1=mx8[:, 1:2], scalar2=0.5,
            op0=OP.is_ge, op1=OP.mult),
            reads=["g8", "mx8"], writes=[("mask", tt)])

    for oc in range(4):
        ocs = slice(512 * oc, 512 * (oc + 1))
        for e in range(8):
            i = e % 2
            P.op("gp", lambda g, e=e, oc=oc, i=i: g.dma_start(
                out=exsl[i],
                in_=expert_w[e, :, 512 * oc:512 * (oc + 1)].rearrange(
                    "(k p) o -> p k o", p=128)),
                writes=[("ex", i)], dma=True)
            P.op("gp", lambda g, e=e, oc=oc, i=i: g.dma_start(
                out=berow[i], in_=expert_b[e:e + 1, 512 * oc:512 * (oc + 1)]),
                writes=[("be", i)], dma=True)
            for tt in range(16):
                k = tt % 2
                for dc in range(16):
                    P.op("pe", lambda t, tt=tt, dc=dc, k=k, i=i: t.matmul(
                        pb[k][:], xnqT[:, dc, 128 * tt:128 * (tt + 1)], exsl[i][:, dc, :],
                        start=(dc == 0), stop=False),
                        reads=[("xnqT", dc), ("ex", i)], writes=[f"pb{k}"])
                P.op("pe", lambda t, k=k, i=i: t.matmul(
                    pb[k][:], ones_sb, berow[i], start=False, stop=True),
                    reads=[("be", i), "ones"], writes=[f"pb{k}"])
                if e == 0:
                    P.op("act", lambda a, tt=tt, k=k: a.activation(
                        out=acc[:, tt, :], in_=pb[k][:], func=AF.Relu,
                        scale=mask_sb[:, tt, 0:1]),
                        reads=[f"pb{k}", ("mask", tt)], writes=[("acc", tt)])
                else:
                    r = tt % 2
                    P.op("act", lambda a, tt=tt, k=k, e=e, r=r: a.activation(
                        out=relutmp[r], in_=pb[k][:], func=AF.Relu,
                        scale=mask_sb[:, tt, e:e + 1]),
                        reads=[f"pb{k}", ("mask", tt)], writes=[("rlt", r)])
                    P.op("dve", lambda v, tt=tt, r=r: v.tensor_add(
                        out=acc[:, tt, :], in0=acc[:, tt, :], in1=relutmp[r]),
                        reads=[("acc", tt), ("rlt", r)], writes=[("acc", tt)])
        for hc in range(16):
            P.op("sync", lambda s, hc=hc, oc=oc: s.dma_start_transpose(
                wot[:, hc, :], wo_bf[512 * oc:512 * (oc + 1), 128 * hc:128 * (hc + 1)]),
                reads=[("bf", "wo")], writes=[("wot", hc)], dma=True)
        P.op("gp", lambda g, ocs=ocs: g.dma_start(
            out=borow, in_=bo.rearrange("(a d) -> a d", a=1)[:, ocs]),
             writes=["borow"], dma=True)
        for j in range(NJ):
            for hc in range(16):
                P.op("sync", lambda s, hc=hc, j=j: s.dma_start(
                    out=oh2sb[:, hc, :],
                    in_=oh2_hbm[128 * hc:128 * (hc + 1), 512 * j:512 * (j + 1)]),
                    reads=[("oh2", hc, j)], writes=[("oh2sb", hc, j)], dma=True)
            for ts_ in range(4):
                tt = 4 * j + ts_
                k = 2 + (tt % 2)
                for hc in range(16):
                    P.op("pe", lambda t, hc=hc, ts_=ts_, k=k, j=j: t.matmul(
                        pb[k][:], oh2sb[:, hc, 128 * ts_:128 * (ts_ + 1)], wot[:, hc, :],
                        start=(hc == 0), stop=False),
                        reads=[("oh2sb", hc, j), ("wot", hc)], writes=[f"pb{k}"])
                P.op("pe", lambda t, k=k: t.matmul(
                    pb[k][:], ones_sb, borow, start=False, stop=True),
                    reads=["borow", "ones"], writes=[f"pb{k}"])
                i = tt % 2
                P.op("sync", lambda s, tt=tt, ocs=ocs, i=i: s.dma_start(
                    out=xres[i], in_=x_own[128 * tt:128 * (tt + 1), ocs]),
                    writes=[("xres", i)], dma=True)
                P.op("dve", lambda v, tt=tt, k=k, i=i: v.tensor_add(
                    out=relutmp[i], in0=pb[k][:], in1=acc[:, tt, :]),
                    reads=[f"pb{k}", ("acc", tt)], writes=[("rlt", i)])
                P.op("dve", lambda v, tt=tt, i=i: v.tensor_add(
                    out=outstg[i], in0=relutmp[i], in1=xres[i]),
                    reads=[("rlt", i), ("xres", i)], writes=[("outstg", i)])
                P.op("gp", lambda g, tt=tt, ocs=ocs, i=i: g.dma_start(
                    out=out_own[128 * tt:128 * (tt + 1), ocs], in_=outstg[i]),
                    reads=[("outstg", i)], writes=[("out", tt, oc)], dma=True)

    P.op("gp", None, reads=[("out", tt, oc) for tt in range(16) for oc in range(4)])

    P.finalize()

    def run_engine(handle, eng):
        for e in P.ev:
            if e["eng"] != eng:
                continue
            for peng, val in e["wl"]:
                handle.wait_ge(sem[peng], val)
            if e["fn"] is not None:
                ins = e["fn"](handle)
                if e["inc"]:
                    ins.then_inc(sem[eng], e["inc"])
            else:
                assert e["inc"] == 0

    with es:
        @block.tensor
        def _(t):
            run_engine(t, "pe")

        @block.scalar
        def _(a):
            run_engine(a, "act")

        @block.vector
        def _(v):
            run_engine(v, "dve")

        @block.sync
        def _(s):
            run_engine(s, "sync")

        @block.gpsimd
        def _(g):
            run_engine(g, "gp")

    return nc


_NC_CACHE = None


def kernel(L0_attn_x, L0_attn_Wq, L0_attn_Wdkv, L0_attn_Wuk, L0_attn_Wuv,
           L0_attn_ln_g, L0_attn_ln_b, L0_attn_Wo_h, L0_attn_Wo, L0_attn_bo,
           L0_ffn_gate_w, L0_ffn_gate_b, L0_ffn_expert_w, L0_ffn_expert_b):
    global _NC_CACHE
    x = np.ascontiguousarray(L0_attn_x, np.float32)
    iota_q = np.broadcast_to(np.arange(512, dtype=np.float16), (128, 512)).copy()
    ones_r = np.ones((1, 128), dtype=BF)
    lngcol = np.ascontiguousarray(
        np.asarray(L0_attn_ln_g, np.float32).reshape(16, 4, 128).transpose(2, 0, 1)
        .reshape(128, 64))
    lnbcol = np.ascontiguousarray(
        np.asarray(L0_attn_ln_b, np.float32).reshape(16, 4, 128).transpose(2, 0, 1)
        .reshape(128, 64))
    shared = {
        "Wq": np.ascontiguousarray(L0_attn_Wq, np.float32),
        "Wdkv": np.ascontiguousarray(L0_attn_Wdkv, np.float32),
        "Wuk": np.ascontiguousarray(L0_attn_Wuk, np.float32),
        "Wuv": np.ascontiguousarray(L0_attn_Wuv, np.float32),
        "Wo_h": np.ascontiguousarray(L0_attn_Wo_h, np.float32),
        "Wo": np.ascontiguousarray(L0_attn_Wo, np.float32),
        "bo": np.ascontiguousarray(L0_attn_bo, np.float32),
        "gate_w": np.ascontiguousarray(L0_ffn_gate_w, np.float32),
        "gate_b": np.ascontiguousarray(L0_ffn_gate_b, np.float32),
        "expert_w": np.ascontiguousarray(L0_ffn_expert_w, np.float32),
        "expert_b": np.ascontiguousarray(L0_ffn_expert_b, np.float32),
        "lngcol": lngcol, "lnbcol": lnbcol,
        "iota_q": iota_q, "ones_r": ones_r,
        "consts": np.tile(np.array([[EPS, -0.5, 0.5, LN_ISQ, 1.0 / L,
                                     -1.0, 1.0, 0.0]], np.float32), (128, 1)),
    }
    in_maps = []
    groups_all = []
    for core in range(8):
        b, ii = core // 4, core % 4
        groups = [ii, 7 - ii, 8 + ii, 15 - ii]
        groups_all.append((b, groups))
        x_own = np.ascontiguousarray(
            np.concatenate([x[b, 512 * g:512 * (g + 1)] for g in groups], axis=0))
        sm = np.empty((128, 64), np.float16)
        p = np.arange(128)
        for j in range(4):
            q0 = 512 * groups[j]
            for kk in range(16):
                t0 = 128 * (16 * j + kk)
                sm[:, 16 * j + kk] = (t0 + p - q0).astype(np.float16)
        im = dict(shared)
        im["x_full"] = x[b]
        im["x_own"] = x_own
        im["smask"] = sm
        in_maps.append(im)

    if _NC_CACHE is None:
        _NC_CACHE = build_nc()
    res = run_bass_kernel_spmd(_NC_CACHE, in_maps, core_ids=list(range(8)))

    out = np.empty((2, S, D), np.float32)
    for core in range(8):
        b, groups = groups_all[core]
        r = res.results[core]["out_own"]
        for j, g in enumerate(groups):
            out[b, 512 * g:512 * (g + 1)] = r[512 * j:512 * (j + 1)]
    return out
